# version 01 (final)
# baseline (speedup 1.0000x reference)
import numpy as np
import ml_dtypes
from contextlib import ExitStack
import concourse.bass as bass
import concourse.mybir as mybir
from concourse.bass_utils import run_bass_kernel_spmd

F32 = mybir.dt.float32
BF16 = mybir.dt.bfloat16
AF = mybir.ActivationFunctionType
ALU = mybir.AluOpType
bf16 = ml_dtypes.bfloat16

E, S, H, D = 1024, 2048, 16, 64
ET, ST = 8, 16
NL = 6
C1N, CI, CT = 5460, 2730, 22
QH, SH = 2, 1024
EPS = 1e-5
LN8 = float(np.log(8.0))
UPTO = "full"


def _rotary_tables():
    half = np.arange(0, D, 2, dtype=np.float64)
    inv_freq = 1.0 / 10000.0 ** (half / D)
    t = np.arange(S, dtype=np.float64)
    freqs = np.concatenate([t[:, None] * inv_freq[None, :]] * 2, -1)
    base_scale = (half + 0.4 * D) / (1.4 * D)
    power = (t - S // 2) / 128.0
    scale = np.concatenate([base_scale[None, :] ** power[:, None]] * 2, -1)
    cos, sin = np.cos(freqs), np.sin(freqs)

    def sg(tb):
        o = tb.copy()
        o[:, :32] = -o[:, :32]
        return o
    return ((cos * scale).astype(np.float32), sg(sin * scale).astype(np.float32),
            (cos / scale).astype(np.float32), sg(sin / scale).astype(np.float32))


def _prep(inp):
    d = {}
    x = np.asarray(inp["L0_attn_x"], np.float32)
    d["_x"] = [np.ascontiguousarray(x[2 * c:2 * c + 2].transpose(0, 2, 1)) for c in range(8)]
    for nm in ["Wq", "Wk", "Wv", "Wo"]:
        d[nm + "T"] = np.ascontiguousarray(np.asarray(inp[f"L0_attn_{nm}"], np.float32).T)
    for nm in ["bq", "bk", "bo"]:
        d[nm] = np.ascontiguousarray(np.asarray(inp[f"L0_attn_{nm}"], np.float32).reshape(ET, 128).T)
    d["bv_bc"] = np.ascontiguousarray(np.broadcast_to(np.asarray(inp["L0_attn_bv"], np.float32)[None, :], (128, E)))
    d["dp_bc"] = np.ascontiguousarray(np.broadcast_to(np.asarray(inp["L0_attn_decay_params"], np.float32).reshape(1, 2 * H), (128, 2 * H)))
    mwf = np.asarray(inp["L0_attn_mix_weights"], np.float32)
    d["mw_bc"] = np.ascontiguousarray(np.broadcast_to(mwf.reshape(1, 2 * H), (128, 2 * H)))
    d["mws_bc"] = np.ascontiguousarray(np.broadcast_to(mwf[:, ::-1] if False else mwf[:, [1, 0]].reshape(1, 2 * H), (128, 2 * H)))
    d["Wqkv"] = np.asarray(inp["L0_ffn_Wqkv"], np.float32)
    d["Wof"] = np.asarray(inp["L0_ffn_Wo"], np.float32)
    d["W1"] = np.asarray(inp["L0_ffn_W1"], np.float32)
    d["W2"] = np.asarray(inp["L0_ffn_W2"], np.float32)
    d["lng"] = np.ascontiguousarray(np.asarray(inp["L0_ffn_ln_g"], np.float32).reshape(NL, ET, 128).transpose(0, 2, 1))
    d["lnb"] = np.ascontiguousarray(np.asarray(inp["L0_ffn_ln_b"], np.float32).reshape(NL, ET, 128).transpose(0, 2, 1))
    b1 = np.asarray(inp["L0_ffn_b1"], np.float32)
    pa = np.zeros((NL, CT * 128), np.float32)
    pg = np.zeros((NL, CT * 128), np.float32)
    pa[:, :CI] = b1[:, :CI]
    pg[:, :CI] = b1[:, CI:]
    d["b1a"] = np.ascontiguousarray(pa.reshape(NL, CT, 128).transpose(0, 2, 1))
    d["b1g"] = np.ascontiguousarray(pg.reshape(NL, CT, 128).transpose(0, 2, 1))
    d["b2"] = np.ascontiguousarray(np.asarray(inp["L0_ffn_b2"], np.float32).reshape(NL, ET, 128).transpose(0, 2, 1))
    qsf = np.asarray(inp["L0_ffn_q_scale"], np.float32)
    ksf = np.asarray(inp["L0_ffn_k_scale"], np.float32)
    d["qs_bc"] = np.ascontiguousarray(np.broadcast_to(qsf[:, None, :], (NL, 128, D)))
    d["ks_bc"] = np.ascontiguousarray(np.broadcast_to(ksf[:, None, :], (NL, 128, D)))
    d["qsr_bc"] = np.ascontiguousarray(np.broadcast_to(np.roll(qsf, -32, -1)[:, None, :], (NL, 128, D)))
    d["ksr_bc"] = np.ascontiguousarray(np.broadcast_to(np.roll(ksf, -32, -1)[:, None, :], (NL, 128, D)))
    d["tcq"], d["tsq"], d["tck"], d["tsk"] = _rotary_tables()
    i = np.arange(128)
    d["tlo"] = np.ascontiguousarray((i[:, None] >= i[None, :]).astype(bf16))
    d["thi"] = np.ascontiguousarray((i[:, None] <= i[None, :]).astype(bf16))
    return d


class Chan:
    def __init__(self, nc, stack, name):
        self.h = stack.enter_context(nc.semaphore(name))
        self.n = 0

    def sig(self, instr, k=1):
        instr.then_inc(self.h, k)
        self.n += k
        return instr

    def wait(self, eng):
        eng.wait_ge(self.h, self.n)

    def at(self, eng, n):
        if n > 0:
            eng.wait_ge(self.h, n)


class KB:
    def __init__(self, upto):
        self.upto = upto
        nc = self.nc = bass.Bass("TRN2", target_bir_lowering=False)
        self.stack = ExitStack()
        di = lambda nm, sh, dt=F32: nc.dram_tensor(nm, sh, dt, kind="ExternalInput").ap()
        self.xT = di("xT", [2, E, S])
        self.Wd = {nm: di(nm, [E, E]) for nm in ["WqT", "WkT", "WvT", "WoT"]}
        self.Bd = {nm: di(nm, [128, ET]) for nm in ["bq", "bk", "bo"]}
        self.bv_d = di("bv_bc", [128, E])
        self.dp_d = di("dp_bc", [128, 2 * H])
        self.mw_d = di("mw_bc", [128, 2 * H])
        self.mws_d = di("mws_bc", [128, 2 * H])
        self.Wqkv_d = di("Wqkv", [NL, E, 3 * E])
        self.Wof_d = di("Wof", [NL, E, E])
        self.W1_d = di("W1", [NL, E, C1N])
        self.W2_d = di("W2", [NL, CI, E])
        self.lng_d = di("lng", [NL, 128, ET])
        self.lnb_d = di("lnb", [NL, 128, ET])
        self.b1a_d = di("b1a", [NL, 128, CT])
        self.b1g_d = di("b1g", [NL, 128, CT])
        self.b2_d = di("b2", [NL, 128, ET])
        self.qs_d = di("qs_bc", [NL, 128, D])
        self.ks_d = di("ks_bc", [NL, 128, D])
        self.qsr_d = di("qsr_bc", [NL, 128, D])
        self.ksr_d = di("ksr_bc", [NL, 128, D])
        self.Td = {nm: di(nm, [S, D]) for nm in ["tcq", "tsq", "tck", "tsk"]}
        self.tlo_d = di("tlo", [128, 128], BF16)
        self.thi_d = di("thi", [128, 128], BF16)
        self.outT = nc.dram_tensor("outT", [2, E, S], F32, kind="ExternalOutput").ap()
        self.x1sp = nc.dram_tensor("x1sp", [2, E, S], F32, kind="Internal").ap()
        self.chans = {}

    def sb(self, nm, sh, dt):
        return self.stack.enter_context(self.nc.sbuf_tensor(nm, sh, dt))

    def ch(self, nm):
        if nm not in self.chans:
            self.chans[nm] = Chan(self.nc, self.stack, nm)
        return self.chans[nm]

    def alloc(self):
        st = self.stack
        self.z = self.sb("z", [128, ET * S], F32)
        self.A48 = self.sb("A48", [128, 24576], BF16)
        self.XB = self.sb("XB", [128, 16384], BF16)
        self.R34 = self.sb("R34", [128, 16640], BF16)
        self.R34f = None
        self.SCf = self.sb("SCf", [128, 5120], F32)
        self.SCb = self.SCf[:, :].bitcast(BF16)
        self.R34f = self.R34[:, :].bitcast(F32)
        self.ones_bf = self.sb("ones_bf", [128, 128], BF16)
        self.idn = self.sb("idn", [128, 128], BF16)
        self.tri = self.sb("tri", [128, 256], BF16)
        self.bias4 = self.sb("bias4", [128, 3 * ET], F32)
        self.lsc = self.sb("lsc", [128, 64], F32)
        self.anat = self.sb("anat", [128, 512], BF16)
        self.rnrm = self.sb("rnrm", [128, 64], F32)
        self.epsc = self.sb("epsc", [128, 1], F32)
        self.mws = self.sb("mws", [128, 32], F32)
        self.lb = self.sb("lb", [128, 70], F32)
        self.ctab = self.sb("ctab", [128, 16 * 256], BF16)
        self.P_SC = st.enter_context(self.nc.psum_tensor("P_SC", [128, 1024], F32))
        self.P_OUT = st.enter_context(self.nc.psum_tensor("P_OUT", [128, 2560], F32))
        self.P_TR = st.enter_context(self.nc.psum_tensor("P_TR", [128, 1024], BF16))
        self.P_Xf = self.P_TR[:, :].bitcast(F32)
        self.blk = self.stack.enter_context(self.nc.Block())

    def scf(self, a, b):
        return self.SCf[:, a:b]

    def scb(self, a, b):
        return self.SCb[:, a:b]

    def zt(self, et, sl=None):
        return self.z[:, et * S:(et + 1) * S] if sl is None else self.z[:, et * S + sl.start:et * S + sl.stop]

    def xbt(self, et, sl=None):
        return self.XB[:, et * S:(et + 1) * S] if sl is None else self.XB[:, et * S + sl.start:et * S + sl.stop]

    def out_slot(self, idx):
        bank, off = divmod(idx, 7)
        o = bank * 512 + off * 65
        return self.P_OUT[:, o:o + 65]

    def psum4(self, parity):
        if parity == 0:
            return [self.P_SC[:, 0:512], self.P_SC[:, 512:1024],
                    self.P_OUT[:, 0:512], self.P_OUT[:, 512:1024]]
        return [self.P_OUT[:, 1024:1536], self.P_OUT[:, 1536:2048],
                self.P_OUT[:, 2048:2560], self.P_Xf[:, 0:512]]

    def wdma(self, g, dram, c0, half):
        dst = self.scb(half * 4096, half * 4096 + 4096).rearrange("p (a f) -> p a f", a=ET)
        src = dram[:, c0:c0 + 512].rearrange("(a p) f -> p a f", p=128)
        return g.dma_start(dst, src)

    def wtile(self, half, et, col):
        o = half * 4096 + et * 512 + col
        return self.SCb[:, o:o + 128]

    def init_consts(self):
        blk = self.blk
        init, mix = self.ch("init"), self.ch("mix")

        @blk.vector
        def _(v):
            v.memset(self.epsc[:], EPS)
            v.memset(self.ones_bf[:], 1.0)
            init.sig(v.drain())

        @blk.gpsimd
        def _(g):
            g.iota(self.idn[:], pattern=[[-1, 128]], base=0, channel_multiplier=1,
                   allow_small_or_imprecise_dtypes=True)
            init.sig(g.drain())

        @blk.sync
        def _(s):
            init.sig(s.dma_start(self.tri[:, 0:128], self.tlo_d[:, :]), 16)
            init.sig(s.dma_start(self.tri[:, 128:256], self.thi_d[:, :]), 16)
            init.sig(s.dma_start(self.bias4[:, 0:ET], self.Bd["bq"][:, :]), 16)
            init.sig(s.dma_start(self.bias4[:, ET:2 * ET], self.Bd["bk"][:, :]), 16)
            init.sig(s.dma_start(self.bias4[:, 2 * ET:3 * ET], self.Bd["bo"][:, :]), 16)
            init.sig(s.dma_start(self.rnrm[:, 0:32], self.dp_d[:, :]), 16)
            init.sig(s.dma_start(self.rnrm[:, 32:64], self.mw_d[:, :]), 16)
            init.sig(s.dma_start(self.mws[:, :], self.mws_d[:, :]), 16)

        @blk.vector
        def _(v):
            init.wait(v)
            v.tensor_scalar(self.idn[:], self.idn[:], 0.0, None, ALU.is_equal)
            v.tensor_scalar(self.lsc[:, 0:32], self.rnrm[:, 0:32], 0.01, 0.9999, ALU.max, ALU.min)
            init.sig(v.drain())

        @blk.scalar
        def _(sc):
            init.wait(sc)
            sc.activation(self.lsc[:, 0:32], self.lsc[:, 0:32], AF.Ln)
            sc.activation(self.lsc[:, 32:64], self.rnrm[:, 32:64], AF.Exp)
            sc.activation(self.mws[:, :], self.mws[:, :], AF.Exp)
            mix.sig(sc.drain())

        @blk.vector
        def _(v):
            mix.wait(v)
            v.tensor_tensor(self.rnrm[:, 0:32], self.lsc[:, 32:64], self.mws[:, :], ALU.add)
            mix.sig(v.drain())

        @blk.scalar
        def _(sc):
            mix.wait(sc)
            sc.activation(self.rnrm[:, 0:32], self.rnrm[:, 0:32], AF.Ln)
            mix.sig(sc.drain())

        @blk.vector
        def _(v):
            mix.wait(v)
            v.tensor_sub(self.lsc[:, 32:64], self.rnrm[:, 32:64], self.rnrm[:, 0:32])
            v.drain()
            v.tensor_scalar(self.lsc[:, 32:64], self.lsc[:, 32:64], LN8, None, ALU.subtract)
            mix.sig(v.drain())


    def load_x(self, sq):
        blk = self.blk
        ld = self.ch("ldx")

        @blk.sync
        def _(s):
            if sq == 1:
                self.ch("store").wait(s)
            for et in range(ET):
                ld.sig(s.dma_start(self.zt(et), self.xT[sq, et * 128:(et + 1) * 128, :]), 16)
        return ld

    def dump_z(self, sq, wait_list):
        blk = self.blk
        wait_list = [w if isinstance(w, Chan) else self.ch(w) for w in wait_list]
        st = self.ch("store")

        @blk.sync
        def _(s):
            for c in wait_list:
                c.wait(s)
            for et in range(ET):
                st.sig(s.dma_start(self.outT[sq, et * 128:(et + 1) * 128, :], self.zt(et)), 16)
            st.wait(s)

    def store_z(self, sq, wait_list):
        blk = self.blk
        wait_list = [w if isinstance(w, Chan) else self.ch(w) for w in wait_list]
        st = self.ch("store")

        @blk.sync
        def _(s):
            for c in wait_list:
                c.wait(s)
            for et in range(ET):
                st.sig(s.dma_start(self.outT[sq, et * 128:(et + 1) * 128, :], self.zt(et)), 16)
            st.wait(s)

    def ln_z_to_xb(self, tag, wait_chs):
        blk = self.blk
        wait_chs = [w if isinstance(w, Chan) else self.ch(w) for w in wait_chs]
        cc, cs, cv = self.ch(tag + "c"), self.ch(tag + "s"), self.ch(tag + "v")
        zc, sqb = self.scb(2048, 3072), self.scb(3072, 4096)
        nm_ap, rs_ap, tmp_ap = self.scf(2048, 3072), self.scf(3072, 4096), self.scf(0, 1024)
        for qh in range(QH):
            sl = slice(qh * SH, (qh + 1) * SH)
            b_cc, b_cs, b_cv = cc.n, cs.n, cv.n

            @blk.scalar
            def _(sc):
                for w in wait_chs:
                    w.wait(sc)
                cv.at(sc, b_cv)
                for et in range(ET):
                    cs.at(sc, b_cs + et)
                    sc.activation(zc, self.zt(et, sl), AF.Identity)
                    sc.activation(sqb, self.zt(et, sl), AF.Square)
                    cc.sig(sc.drain(), 2)

            @blk.tensor
            def _(t):
                for et in range(ET):
                    t.wait_ge(cc.h, b_cc + 2 * et + 2)
                    t.matmul(self.P_SC[:, 0:512], self.ones_bf[:], zc[:, 0:512], start=(et == 0), stop=(et == ET - 1))
                    t.matmul(self.P_SC[:, 512:1024], self.ones_bf[:], zc[:, 512:1024], start=(et == 0), stop=(et == ET - 1))
                    t.matmul(self.P_OUT[:, 0:512], self.ones_bf[:], sqb[:, 0:512], start=(et == 0), stop=(et == ET - 1))
                    t.matmul(self.P_OUT[:, 512:1024], self.ones_bf[:], sqb[:, 512:1024], start=(et == 0), stop=(et == ET - 1))
                    cs.sig(t.drain())

            @blk.vector
            def _(v):
                cs.at(v, b_cs + ET)
                v.tensor_scalar(nm_ap, self.P_SC[:, 0:1024], -1.0 / E, None, ALU.mult)
                v.drain()
                v.tensor_tensor(rs_ap, nm_ap, nm_ap, ALU.mult)
                v.drain()
                v.scalar_tensor_tensor(rs_ap, self.P_OUT[:, 0:1024], 1.0 / E, rs_ap, ALU.mult, ALU.subtract)
                cv.sig(v.drain())

            @blk.scalar
            def _(sc):
                cv.at(sc, b_cv + 1)
                sc.activation(rs_ap, rs_ap, AF.Sqrt, bias=self.epsc[:, 0:1])
                cv.sig(sc.drain())

            @blk.vector
            def _(v):
                cv.at(v, b_cv + 2)
                v.reciprocal(rs_ap, rs_ap)
                v.drain()
                for et in range(ET):
                    v.tensor_tensor(tmp_ap, self.zt(et, sl), nm_ap, ALU.add)
                    v.drain()
                    v.tensor_tensor(self.xbt(et, sl), tmp_ap, rs_ap, ALU.mult)
                cv.sig(v.drain())
        return cv

    def stage_g(self, sq, ln0):
        blk = self.blk
        if self.upto == "lsc":
            blk2 = self.blk
            stc = self.ch("store")

            @blk2.sync
            def _(s):
                self.ch("mix").wait(s)
                stc.sig(s.dma_start(self.outT[0, 0:128, 0:64], self.lsc[:, :]), 16)
                stc.wait(s)
            return
        if self.upto == "ln0":
            self.dump_bf16(sq, lambda et: self.xbt(et), [ln0])
            return
        self._proj_T("kP", "WkT", bias_col=1, dst_base=0, dst_w=2048, sl=slice(0, S),
                     pre_wait=[ln0], extra_dma=("bv",))
        if self.upto == "kproj":
            self.dump_bf16(sq, lambda et: self.A48[:, et * 2048:(et + 1) * 2048], ["kPE"])
            return
        self._v_proj(sq)
        if self.upto == "vproj":
            self.dump_bf16(sq, lambda et: self.R34[:, et * 2080:et * 2080 + 2048], ["vE"])
            return
        for qh in range(QH):
            self._proj_T(f"qh{qh}", "WqT", bias_col=0, dst_base=16384, dst_w=1024,
                         sl=slice(qh * SH, (qh + 1) * SH),
                         pre_wait=["vM", "vE"] if qh == 0
                         else ["ath0sm", "ath0pv", "ath0ts", "qh0M"])
            if self.upto == "qproj" and qh == 0:
                self.dump_bf16(sq, lambda et: self.A48[:, 16384 + et * 1024: 16384 + (et + 1) * 1024],
                               ["qh0E"], width=1024)
                return
            self._attn_half(sq, qh)
            if self.upto == "h14":
                return
        if self.upto == "attn":
            self.dump_bf16(sq, lambda et: self.xbt(et), ["ath0ev", "ath1ev"])
            return
        self._wo_ln_res(sq)

    def _proj_T(self, tag, wnm, bias_col, dst_base, dst_w, sl, pre_wait, extra_dma=()):
        blk = self.blk
        pre_wait = [c if isinstance(c, Chan) else self.ch(c) for c in pre_wait]
        gW, mm, ev = self.ch(tag + "W"), self.ch(tag + "M"), self.ch(tag + "E")
        nsc = dst_w // 512
        b_gW, b_mm, b_ev = gW.n, mm.n, ev.n

        @blk.gpsimd
        def _(g):
            for c in pre_wait:
                c.wait(g)
            gW.sig(self.wdma(g, self.Wd[wnm], 0, 0), 16)
            gW.sig(self.wdma(g, self.Wd[wnm], 512, 1), 16)
            if "bv" in extra_dma:
                gW.sig(g.dma_start(self.scf(4096, 5120), self.bv_d[:, :]), 16)

        @blk.tensor
        def _(t):
            t.wait_ge(gW.h, b_gW + 32)
            for c in pre_wait:
                c.wait(t)
            for ft in range(ET):
                ps4 = self.psum4(ft % 2)[0:nsc]
                ev.at(t, b_ev + ft - 1)
                for et in range(ET):
                    wch, col = (0, ft * 128) if ft < 4 else (1, ft * 128 - 512)
                    lhs = self.wtile(wch, et, col)
                    for j in range(nsc):
                        i_ = t.matmul(ps4[j], lhs, self.xbt(et, slice(sl.start + j * 512, sl.start + j * 512 + 512)),
                                      start=(et == 0), stop=(et == ET - 1))
                mm.sig(t.drain())

        @blk.scalar
        def _(sc):
            for ft in range(ET):
                ps4 = self.psum4(ft % 2)[0:nsc]
                mm.at(sc, b_mm + ft + 1)
                dst = self.A48[:, dst_base + ft * dst_w: dst_base + ft * dst_w + dst_w]
                for j in range(nsc):
                    sc.activation(dst[:, j * 512:j * 512 + 512], ps4[j], AF.Identity,
                                  bias=self.bias4[:, bias_col * ET + ft:bias_col * ET + ft + 1])
                ev.sig(sc.drain())

    def _v_proj(self, sq):
        blk = self.blk
        gW, mm, ev = self.ch("vW"), self.ch("vM"), self.ch("vE")
        kdone, kev, kw = self.ch("kPM"), self.ch("kPE"), self.ch("kPW")
        b_gW, b_mm, b_ev = gW.n, mm.n, ev.n

        @blk.gpsimd
        def _(g):
            kdone.wait(g)
            gW.sig(self.wdma(g, self.Wd["WvT"], 0, 0), 16)
            gW.sig(self.wdma(g, self.Wd["WvT"], 512, 1), 16)

        @blk.vector
        def _(v):
            va = self.R34[:, 0:16640].rearrange("p (st h c) -> p st h c", st=ST, h=H, c=65)
            v.memset(va[:, :, :, 64:65], 1.0)
            ev.sig(v.drain())
        b_ev += 1

        @blk.tensor
        def _(t):
            t.wait_ge(gW.h, b_gW + 32)
            kev.wait(t)
            for st in range(ST):
                psA = self.P_SC[:, 0:1024] if st % 2 == 0 else self.P_OUT[:, 0:1024]
                ev.at(t, b_ev + st - 1)
                for et in range(ET):
                    lhs = self.XB[:, et * 2048 + st * 128: et * 2048 + st * 128 + 128]
                    t.matmul(psA[:, 0:512], lhs, self.scb(0, 4096)[:, et * 512:et * 512 + 512],
                             start=(et == 0), stop=(et == ET - 1))
                    i_ = t.matmul(psA[:, 512:1024], lhs, self.scb(4096, 8192)[:, et * 512:et * 512 + 512],
                                  start=(et == 0), stop=(et == ET - 1))
                mm.sig(t.drain())

        @blk.vector
        def _(v):
            kw.at(v, 48)
            bv = self.scf(4096, 5120).rearrange("p (h c) -> p h c", h=H, c=64)
            for st in range(ST):
                psA = self.P_SC[:, 0:1024] if st % 2 == 0 else self.P_OUT[:, 0:1024]
                mm.at(v, b_mm + st + 1)
                dst = self.R34[:, st * 1040: st * 1040 + 1040].rearrange("p (h c) -> p h c", h=H, c=65)[:, :, 0:64]
                v.tensor_tensor(dst, psA.rearrange("p (h c) -> p h c", h=H, c=64), bv, ALU.add)
                ev.sig(v.drain())

    def _attn_half(self, sq, qh):
        blk = self.blk
        tag = f"ath{qh}"
        aD = self.ch(tag + "D")
        aM = self.ch(tag + "M")
        aE1 = self.ch(tag + "e")
        aQK = self.ch(tag + "qk")
        aSM = self.ch(tag + "sm")
        aPT = self.ch(tag + "pt")
        aPV = self.ch(tag + "pv")
        aTS = self.ch(tag + "ts")
        aTR = self.ch(tag + "tr")
        aEV = self.ch(tag + "ev")
        aIO = self.ch(tag + "io")
        Dab, M, Mt, SM = self.scf(0, 1024), self.scf(1024, 2048), self.scf(2048, 3072), self.scf(3072, 4096)
        PT = lambda par: self.scb(8192 + par * 1024, 8192 + par * 1024 + 1024)
        qdone, vdone = self.ch(f"qh{qh}E"), self.ch("vE")
        bD, bM, bE1, bQK, bSM, bPT, bPV, bTS, bTR, bEV = (
            aD.n, aM.n, aE1.n, aQK.n, aSM.n, aPT.n, aPV.n, aTS.n, aTR.n, aEV.n)

        for gi, hg in enumerate(range(4)):
            for kjt in range(ST):
                it = gi * ST + kjt
                C = -1024 * qh + 128 * kjt

                @blk.gpsimd
                def _(g):
                    if hg == 0 and kjt == 0:
                        qdone.wait(g)
                        vdone.wait(g)
                    aE1.at(g, bE1 + it * 4)
                    g.iota(Dab, pattern=[[-1, 1024]], base=C, channel_multiplier=1,
                           allow_small_or_imprecise_dtypes=True)
                    aIO.sig(g.drain())

                @blk.scalar
                def _(sc):
                    aIO.wait(sc)
                    sc.activation(Dab, Dab, AF.Abs)
                    aD.sig(sc.drain())

                for hj in range(4):
                    h = hg * 4 + hj
                    i = it * 4 + hj
                    par = i % 2
                    ft, row = h // 2, 64 * (h % 2)

                    @blk.scalar
                    def _(sc):
                        aD.at(sc, bD + it + 1)
                        aSM.at(sc, bSM + i)
                        sc.activation(M, Dab, AF.Exp, scale=self.lsc[:, 2 * h:2 * h + 1],
                                      bias=self.lsc[:, 32 + 2 * h:33 + 2 * h])
                        sc.activation(Mt, Dab, AF.Exp, scale=self.lsc[:, 2 * h + 1:2 * h + 2],
                                      bias=self.lsc[:, 33 + 2 * h:34 + 2 * h])
                        aE1.sig(sc.drain())

                    @blk.vector
                    def _(v):
                        aE1.at(v, bE1 + i + 1)
                        v.tensor_tensor(M, M, Mt, ALU.add)
                        aM.sig(v.drain())

                    @blk.tensor
                    def _(t):
                        if gi == 0 and kjt == 0 and hj == 0:
                            qdone.wait(t)
                            vdone.wait(t)
                            self.ch("kPE").wait(t)
                        aSM.at(t, bSM + i)
                        lhs = self.A48[row:row + 64, ft * 2048 + kjt * 128: ft * 2048 + kjt * 128 + 128]
                        rhs = self.A48[row:row + 64, 16384 + ft * 1024: 16384 + ft * 1024 + 1024]
                        t.matmul(self.P_SC[:, 0:512], lhs, rhs[:, 0:512], start=True, stop=True)
                        t.matmul(self.P_SC[:, 512:1024], lhs, rhs[:, 512:1024], start=True, stop=True)
                        aQK.sig(t.drain())

                    @blk.vector
                    def _(v):
                        aQK.at(v, bQK + i + 1)
                        aM.at(v, bM + i + 1)
                        aPT.at(v, bPT + i)
                        v.scalar_tensor_tensor(SM, self.P_SC[:, 0:1024], 240.0, M, ALU.min, ALU.mult)
                        aSM.sig(v.drain())

                    @blk.scalar
                    def _(sc):
                        aSM.at(sc, bSM + i + 1)
                        aPV.at(sc, bPV + i - 1)
                        sc.activation(PT(par), SM, AF.Exp)
                        aPT.sig(sc.drain())

                    if self.upto == "h14" and qh == 0 and hg == 3 and kjt == 0 and hj == 2:
                        st_ = self.ch("store")

                        @blk.scalar
                        def _(sc):
                            aPT.at(sc, bPT + i + 1)
                            sc.activation(self.scf(5120, 5184), self.lsc[:, :], AF.Identity)
                            self.ch("lscd").sig(sc.drain())

                        @blk.sync
                        def _(s):
                            aPT.at(s, bPT + i + 1)
                            self.ch("lscd").wait(s)
                            st_.sig(s.dma_start(self.outT[0, 512:640, 0:64], self.scf(5120, 5184)), 16)
                            st_.sig(s.dma_start(self.outT[0, 640:768, 0:64], self.scf(5184, 5248)), 16)
                            st_.sig(s.dma_start(self.outT[0, 0:128, 0:1024], M), 16)
                            st_.sig(s.dma_start(self.outT[0, 128:256, 0:1024], SM), 16)
                            st_.sig(s.dma_start(self.outT[0, 256:384, 0:1024], Dab), 16)
                            st_.sig(s.dma_start(self.outT[0, 384:512, 0:512], PT(par).bitcast(F32)), 16)
                            st_.wait(s)
                        return

                    @blk.tensor
                    def _(t):
                        aPT.at(t, bPT + i + 1)
                        if kjt == 0 and gi > 0:
                            aTS.at(t, bTS + gi * 4)
                        for qt in range(8):
                            idx = hj * 8 + qt
                            i_ = t.matmul(self.out_slot(idx), PT(par)[:, qt * 128:qt * 128 + 128],
                                          self.R34[:, kjt * 1040 + 65 * h: kjt * 1040 + 65 * h + 65],
                                          start=(kjt == 0 and idx % 7 == 0), stop=(kjt == ST - 1))
                        aPV.sig(t.drain())

            for hj in range(4):
                h = hg * 4 + hj

                @blk.vector
                def _(v):
                    aPV.at(v, bPV + (gi + 1) * 4 * ST)
                    aTR.at(v, bTR + gi * 4 + hj)
                    for qt in range(8):
                        sl_ = self.out_slot(hj * 8 + qt)
                        v.reciprocal(self.rnrm[:, qt:qt + 1], sl_[:, 64:65])
                        v.drain()
                        v.tensor_scalar(self.anat[:, qt * 64:qt * 64 + 64], sl_[:, 0:64],
                                        self.rnrm[:, qt:qt + 1], None, ALU.mult)
                    aTS.sig(v.drain())

                @blk.tensor
                def _(t):
                    aTS.at(t, bTS + gi * 4 + hj + 1)
                    aEV.at(t, bEV + gi * 4 + hj)
                    for qt in range(8):
                        i_ = t.transpose(self.P_TR[0:64, qt * 128:qt * 128 + 128],
                                         self.anat[:, qt * 64:qt * 64 + 64], self.idn[:])
                    aTR.sig(t.drain())

                @blk.scalar
                def _(sc):
                    aTR.at(sc, bTR + gi * 4 + hj + 1)
                    row = 64 * (h % 2)
                    dst = self.XB[row:row + 64, (h // 2) * 2048 + qh * 1024: (h // 2) * 2048 + qh * 1024 + 1024]
                    sc.activation(dst, self.P_TR[0:64, 0:1024], AF.Identity)
                    aEV.sig(sc.drain())


    def dump_bf16(self, sq, src_fn, wait_list, width=S):
        blk = self.blk
        wait_list = [w if isinstance(w, Chan) else self.ch(w) for w in wait_list]
        st = self.ch("store")

        @blk.gpsimd
        def _(g):
            for c in wait_list:
                c.wait(g)
            for et in range(ET):
                st.sig(g.dma_start(self.outT[sq, et * 128:(et + 1) * 128, 0:width], src_fn(et)), 16)
            st.wait(g)


    def layer_attn(self, sq, li):
        blk = self.blk
        t = "Ac"
        cW = self.ch(t + "W")
        cA = self.ch(t + "a")
        cV = self.ch(t + "v")
        cT = self.ch(t + "t")
        prev = self.ch("Fcz")

        @blk.gpsimd
        def _(g):
            prev.wait(g)
            cV.wait(g)
            cA.wait(g)
            for et in range(ET):
                cW.sig(g.dma_start(self.A48[:, et * 3072:(et + 1) * 3072],
                                   self.Wqkv_d[li, et * 128:(et + 1) * 128, :]), 16)

        @blk.sync
        def _(s):
            prev.wait(s)
            cV.wait(s)
            cA.wait(s)
            for j, nm in enumerate(["tcq", "tsq", "tck", "tsk"]):
                cW.sig(s.dma_start(self.scf(j * 1024, j * 1024 + 1024).rearrange("p (st d) -> p st d", st=ST),
                                   self.Td[nm].rearrange("(st p) d -> p st d", p=128)), 16)
            cW.sig(s.dma_start(self.scf(4096, 4160), self.qs_d[li, :, :]), 16)
            cW.sig(s.dma_start(self.scf(4160, 4224), self.ks_d[li, :, :]), 16)
            cW.sig(s.dma_start(self.scf(4224, 4288), self.qsr_d[li, :, :]), 16)
            cW.sig(s.dma_start(self.scf(4288, 4352), self.ksr_d[li, :, :]), 16)
            cW.sig(s.dma_start(self.lb[:, 0:8], self.lng_d[li, :, :]), 16)
            cW.sig(s.dma_start(self.lb[:, 8:16], self.lnb_d[li, :, :]), 16)
            cW.sig(s.dma_start(self.lb[:, 16:24], self.b2_d[li, :, :]), 16)
            cW.sig(s.dma_start(self.lb[:, 24:46], self.b1a_d[li, :, :]), 16)
            cW.sig(s.dma_start(self.lb[:, 46:68], self.b1g_d[li, :, :]), 16)

        @blk.vector
        def _(v):
            cW.wait(v)
            ct3 = self.ctab[:, :].rearrange("p (st tb d) -> p st tb d", st=ST, tb=4, d=D)
            for tb, so in [(0, 4096), (1, 4224), (2, 4160), (3, 4288)]:
                stg = self.scf(tb * 1024, tb * 1024 + 1024).rearrange("p (st d) -> p st d", st=ST)
                v.tensor_tensor(ct3[:, :, tb, :], stg,
                                self.scf(so, so + 64).unsqueeze(1).broadcast_to([128, ST, D]), ALU.mult)
            va = self.R34[:, 6144:6144 + 4 * 1040].rearrange("p (j h c) -> p j h c", j=4, h=H, c=65)
            v.memset(va[:, :, :, 64:65], 1.0)
            cV.sig(v.drain())

        if self.upto == "l1rg":
            self._ila_st(sq, li, 0)
            st_ = self.ch("store")

            @blk.gpsimd
            def _(g):
                cA.wait(g)
                cV.wait(g)
                cT.wait(g)
                st_.sig(g.dma_start(self.outT[sq, 0:128, 0:1024], self.R34[:, 0:1024]), 16)
                st_.sig(g.dma_start(self.outT[sq, 128:256, 0:1024], self.R34[:, 4096:5120]), 16)
                st_.sig(g.dma_start(self.outT[sq, 256:384, 0:1040], self.R34[:, 6144:7184]), 16)
                st_.sig(g.dma_start(self.outT[sq, 384:512, 0:1024], self.SCb[:, 8192:9216]), 16)
                st_.sig(g.dma_start(self.outT[sq, 512:640, 0:1024], self.R34[:, 10304:11328]), 16)
                st_.wait(g)
            return

        for st in range(ST):
            self._ila_st(sq, li, st)

        if self.upto == "l1ao":
            st_ = self.ch("store")

            @blk.gpsimd
            def _(g):
                cA.wait(g)
                cV.wait(g)
                cT.wait(g)
                for et in range(ET):
                    st_.sig(g.dma_start(self.outT[sq, et * 128:(et + 1) * 128, :], self.xbt(et)), 16)
                st_.wait(g)
            return

        @blk.gpsimd
        def _(g):
            cV.wait(g)
            cT.wait(g)
            cW.sig(self.wdma_w(g, self.Wof_d[li], 0, 0, 512), 16)
            cW.sig(self.wdma_w(g, self.Wof_d[li], 512, 4096, 512), 16)

        cOM, cOE = self.ch(t + "om"), self.ch(t + "oe")
        bOM, bOE, bW2 = cOM.n, cOE.n, cW.n

        @blk.tensor
        def _(t_):
            t_.wait_ge(cW.h, bW2)
            cA.wait(t_)
            for ft in range(ET):
                cOE.at(t_, bOE + ft)
                for et in range(ET):
                    wch, col = (0, ft * 128) if ft < 4 else (1, ft * 128 - 512)
                    lhs = self.SCb[:, wch * 4096 + et * 512 + col: wch * 4096 + et * 512 + col + 128]
                    t_.matmul(self.P_SC[:, 0:512], lhs, self.xbt(et, slice(0, 512)),
                              start=(et == 0), stop=(et == ET - 1))
                    t_.matmul(self.P_SC[:, 512:1024], lhs, self.xbt(et, slice(512, 1024)),
                              start=(et == 0), stop=(et == ET - 1))
                    t_.matmul(self.P_OUT[:, 0:512], lhs, self.xbt(et, slice(1024, 1536)),
                              start=(et == 0), stop=(et == ET - 1))
                    i_ = t_.matmul(self.P_OUT[:, 512:1024], lhs, self.xbt(et, slice(1536, 2048)),
                                   start=(et == 0), stop=(et == ET - 1))
                cOM.sig(t_.drain())

        @blk.vector
        def _(v):
            for ft in range(ET):
                cOM.at(v, bOM + ft + 1)
                v.tensor_tensor(self.zt(ft, slice(0, 1024)), self.zt(ft, slice(0, 1024)),
                                self.P_SC[:, 0:1024], ALU.add)
                v.tensor_tensor(self.zt(ft, slice(1024, 2048)), self.zt(ft, slice(1024, 2048)),
                                self.P_OUT[:, 0:1024], ALU.add)
                cOE.sig(v.drain())

    def wdma_w(self, g, dram2d, c0, dstoff_b, width):
        dst = self.SCb[:, dstoff_b:dstoff_b + ET * width].rearrange("p (a f) -> p a f", a=ET)
        src = dram2d[:, c0:c0 + width].rearrange("(a p) f -> p a f", p=128)
        return g.dma_start(dst, src)

    def _ila_st(self, sq, li, st):
        blk = self.blk
        t = "Ac"
        cW, cA, cV, cT = self.ch(t + "W"), self.ch(t + "a"), self.ch(t + "v"), self.ch(t + "t")
        t1, t2, t3 = self.scf(0, 1024), self.scf(1024, 2048), self.scf(2048, 3072)
        sqb = self.scb(6144, 7168)
        zs = self.R34[:, 10304 + (st % 2) * 1024: 10304 + (st % 2) * 1024 + 1024]

        if st == 0:
            @blk.vector
            def _(v):
                cW.wait(v)
                for et in range(4, ET):
                    v.memset(self.R34[:, 10304 + et * 128: 10304 + et * 128 + 1], 0.0)
                cV.sig(v.drain())

        @blk.scalar
        def _(sc):
            cV.wait(sc)
            cT.wait(sc)
            for et in range(ET):
                dst = zs[:, et * 128:(et + 1) * 128]
                if et < 4:
                    sc.activation(dst, self.zt(et, slice(st * 128, st * 128 + 128)), AF.Identity)
                elif st == 0:
                    sc.activation(dst[:, 1:128], self.z[:, et * S: et * S + 127], AF.Identity)
                else:
                    sc.activation(dst, self.z[:, et * S + st * 128 - 1: et * S + st * 128 + 127], AF.Identity)
            cA.sig(sc.drain())

        @blk.tensor
        def _(t_):
            cA.wait(t_)
            cV.wait(t_)
            for et in range(ET):
                lhs = zs[:, et * 128:(et + 1) * 128]
                for part, ps in [(0, self.P_SC[:, 0:1024]), (1, self.P_OUT[:, 0:1024]), (2, self.P_OUT[:, 1024:2048])]:
                    base = et * 3072 + part * 1024
                    t_.matmul(ps[:, 0:512], lhs, self.A48[:, base:base + 512],
                              start=(et == 0), stop=(et == ET - 1))
                    i_ = t_.matmul(ps[:, 512:1024], lhs, self.A48[:, base + 512:base + 1024],
                                   start=(et == 0), stop=(et == ET - 1))
            cT.sig(t_.drain())

        for ti, (ps, soff, ctb) in enumerate([(self.P_SC[:, 0:1024], 4096, 0), (self.P_OUT[:, 0:1024], 4160, 2)]):
            rot_dst = self.scb(7168 + 0, 7168 + 1024) if False else self.scb(8192 + ti * 1024, 8192 + (ti + 1) * 1024)

            @blk.scalar
            def _(sc):
                cT.wait(sc)
                cV.wait(sc)
                sc.activation(sqb, ps, AF.Square)
                cA.sig(sc.drain())

            @blk.vector
            def _(v):
                cA.wait(v)
                v.tensor_reduce(self.rnrm[:, 32:48], sqb.rearrange("p (h d) -> p h d", h=H),
                                mybir.AxisListType.X, ALU.add)
                cV.sig(v.drain())

            @blk.scalar
            def _(sc):
                cV.wait(sc)
                sc.activation(self.rnrm[:, 32:48], self.rnrm[:, 32:48], AF.Sqrt)
                cA.sig(sc.drain())

            @blk.vector
            def _(v):
                cA.wait(v)
                v.tensor_scalar(self.rnrm[:, 32:48], self.rnrm[:, 32:48], 1e-12, None, ALU.max)
                v.drain()
                v.reciprocal(self.rnrm[:, 32:48], self.rnrm[:, 32:48])
                v.drain()
                c1 = self.ctab[:, st * 256 + ctb * 64: st * 256 + ctb * 64 + 64]
                c2 = self.ctab[:, st * 256 + (ctb + 1) * 64: st * 256 + (ctb + 1) * 64 + 64]
                ps3 = ps.rearrange("p (h d) -> p h d", h=H)
                v.tensor_tensor(t1.rearrange("p (h d) -> p h d", h=H), ps3,
                                c1.unsqueeze(1).broadcast_to([128, H, D]), ALU.mult)
                v.tensor_tensor(t2.rearrange("p (h d) -> p h d", h=H)[:, :, 0:32],
                                ps.rearrange("p (h d) -> p h d", h=H)[:, :, 32:64],
                                c2.unsqueeze(1).broadcast_to([128, H, D])[:, :, 0:32], ALU.mult)
                v.tensor_tensor(t2.rearrange("p (h d) -> p h d", h=H)[:, :, 32:64],
                                ps.rearrange("p (h d) -> p h d", h=H)[:, :, 0:32],
                                c2.unsqueeze(1).broadcast_to([128, H, D])[:, :, 32:64], ALU.mult)
                v.drain()
                v.tensor_tensor(t3, t1, t2, ALU.add)
                v.drain()
                for h in range(H):
                    v.tensor_scalar(rot_dst[:, h * 64:(h + 1) * 64], t3[:, h * 64:(h + 1) * 64],
                                    self.rnrm[:, 32 + h:33 + h], None, ALU.mult)
                cV.sig(v.drain())

        @blk.scalar
        def _(sc):
            cT.wait(sc)
            vslot = self.R34[:, 6144 + (st % 4) * 1040: 6144 + (st % 4) * 1040 + 1040]
            dst = vslot.rearrange("p (h c) -> p h c", h=H, c=65)[:, :, 0:64]
            sc.activation(dst, self.P_OUT[:, 1024:2048].rearrange("p (h c) -> p h c", h=H, c=64), AF.Identity)
            cA.sig(sc.drain())

        for ti, slot_ap in enumerate([self.R34[:, 4096:5120],
                                      self.R34[:, (st % 4) * 1024: (st % 4) * 1024 + 1024]]):
            @blk.tensor
            def _(t_):
                cV.wait(t_)
                cA.wait(t_)
                src = self.scb(8192 + ti * 1024, 8192 + (ti + 1) * 1024)
                for et in range(ET):
                    i_ = t_.transpose(self.P_TR[:, et * 128:(et + 1) * 128],
                                      src[:, et * 128:(et + 1) * 128], self.idn[:])
                cT.sig(t_.drain())

            @blk.scalar
            def _(sc):
                cT.wait(sc)
                sc.activation(slot_ap, self.P_TR[:, 0:1024], AF.Identity)
                cA.sig(sc.drain())

        self._ila_attn(sq, st)

    def _ila_attn(self, sq, qt):
        blk = self.blk
        t = "Ac"
        cA, cV, cT = self.ch(t + "a"), self.ch(t + "v"), self.ch(t + "t")
        kjs = [k for k in (qt - 2, qt - 1, qt) if k >= 0]
        r0 = 3 - len(kjs)
        PTl = lambda par: self.scb(7168 + par * 512, 7168 + par * 512 + 384)
        for h in range(H):
            ft, row = h // 2, 64 * (h % 2)
            sc_ps = self.P_OUT[:, 2048:2432]

            @blk.tensor
            def _(t_):
                cA.wait(t_)
                cV.wait(t_)
                for j, kj in enumerate(kjs):
                    r = r0 + j
                    kslot = self.R34[:, (kj % 4) * 1024: (kj % 4) * 1024 + 1024]
                    lhs = kslot[row:row + 64, ft * 128: ft * 128 + 128]
                    rhs = self.R34[row:row + 64, 4096 + ft * 128: 4096 + ft * 128 + 128]
                    i_ = t_.matmul(sc_ps[:, r * 128:(r + 1) * 128], lhs, rhs, start=True, stop=True)
                cT.sig(t_.drain())

            @blk.scalar
            def _(sc):
                cT.wait(sc)
                sc.activation(PTl(h % 2)[:, r0 * 128:384], sc_ps[:, r0 * 128:384], AF.Exp, scale=8.0)
                cA.sig(sc.drain())

            @blk.vector
            def _(v):
                cA.wait(v)
                if r0 <= 0:
                    v.tensor_tensor(PTl(h % 2)[:, 0:128], PTl(h % 2)[:, 0:128], self.tri[:, 0:128], ALU.mult)
                v.tensor_tensor(PTl(h % 2)[:, 256:384], PTl(h % 2)[:, 256:384], self.tri[:, 128:256], ALU.mult)
                cV.sig(v.drain())

            @blk.tensor
            def _(t_):
                cV.wait(t_)
                slot = self.out_slot_local(h)
                for j, kj in enumerate(kjs):
                    r = r0 + j
                    vslot = self.R34[:, 6144 + (kj % 4) * 1040: 6144 + (kj % 4) * 1040 + 1040]
                    i_ = t_.matmul(slot, PTl(h % 2)[:, r * 128:(r + 1) * 128],
                                   vslot[:, 65 * h: 65 * h + 65],
                                   start=(j == 0 and h % 7 == 0), stop=(j == len(kjs) - 1))
                cT.sig(t_.drain())

            @blk.vector
            def _(v):
                cT.wait(v)
                slot = self.out_slot_local(h)
                v.reciprocal(self.rnrm[:, 63:64], slot[:, 64:65])
                v.drain()
                v.tensor_scalar(self.anat[:, (h % 8) * 64: (h % 8) * 64 + 64], slot[:, 0:64],
                                self.rnrm[:, 63:64], None, ALU.mult)
                cV.sig(v.drain())

            if h % 8 == 7:
                @blk.tensor
                def _(t_):
                    cV.wait(t_)
                    cA.wait(t_)
                    for j in range(8):
                        i_ = t_.transpose(self.P_TR[0:64, j * 128:(j + 1) * 128],
                                          self.anat[:, j * 64:(j + 1) * 64], self.idn[:])
                    cT.sig(t_.drain())

                @blk.scalar
                def _(sc):
                    cT.wait(sc)
                    hb = h - 7
                    for par in range(2):
                        for k in range(4):
                            hh = hb + 2 * k + par
                            d1 = self.XB[64 * par:64 * par + 64,
                                         (hh // 2) * 2048 + qt * 128: (hh // 2) * 2048 + qt * 128 + 128]
                            sc.activation(d1, self.P_TR[0:64, (2 * k + par) * 128: (2 * k + par) * 128 + 128],
                                          AF.Identity)
                    cA.sig(sc.drain())

    def out_slot_local(self, h):
        bank, off = divmod(h, 7)
        regions = [self.P_SC[:, 0:512], self.P_SC[:, 512:1024], self.P_OUT[:, 0:512]]
        return regions[bank][:, off * 65: off * 65 + 65]


    def spill_ln3(self, sq):
        blk = self.blk
        t = "Fc"
        cSP, cA, cV, cT, cZ = (self.ch(t + "sp"), self.ch(t + "a"), self.ch(t + "v"),
                               self.ch(t + "t"), self.ch(t + "z"))
        wo_done = self.ch("oGZ")

        @blk.sync
        def _(s):
            wo_done.wait(s)
            for et in range(ET):
                cSP.sig(s.dma_start(self.x1sp[sq, et * 128:(et + 1) * 128, :], self.zt(et)), 16)

        zcb, sqb = self.scb(8192, 9216), self.scb(9216, 10240)
        nm_ap, rs_ap, tmp_ap = self.scf(2048, 3072), self.scf(3072, 4096), self.scf(0, 1024)
        for qh in range(QH):
            sl = slice(qh * SH, (qh + 1) * SH)
            b_a, b_t, b_v = cA.n, cT.n, cV.n

            @blk.scalar
            def _(sc):
                wo_done.wait(sc)
                cV.wait(sc)
                for et in range(ET):
                    cT.at(sc, b_t + et)
                    sc.activation(zcb, self.zt(et, sl), AF.Identity)
                    sc.activation(sqb, self.zt(et, sl), AF.Square)
                    cA.sig(sc.drain(), 2)

            @blk.tensor
            def _(t_):
                for et in range(ET):
                    t_.wait_ge(cA.h, b_a + 2 * et + 2)
                    t_.matmul(self.P_SC[:, 0:512], self.ones_bf[:], zcb[:, 0:512], start=(et == 0), stop=(et == ET - 1))
                    t_.matmul(self.P_SC[:, 512:1024], self.ones_bf[:], zcb[:, 512:1024], start=(et == 0), stop=(et == ET - 1))
                    t_.matmul(self.P_OUT[:, 0:512], self.ones_bf[:], sqb[:, 0:512], start=(et == 0), stop=(et == ET - 1))
                    t_.matmul(self.P_OUT[:, 512:1024], self.ones_bf[:], sqb[:, 512:1024], start=(et == 0), stop=(et == ET - 1))
                    cT.sig(t_.drain())

            @blk.vector
            def _(v):
                cT.at(v, b_t + ET)
                v.tensor_scalar(nm_ap, self.P_SC[:, 0:1024], -1.0 / E, None, ALU.mult)
                v.drain()
                v.tensor_tensor(rs_ap, nm_ap, nm_ap, ALU.mult)
                v.drain()
                v.scalar_tensor_tensor(rs_ap, self.P_OUT[:, 0:1024], 1.0 / E, rs_ap, ALU.mult, ALU.subtract)
                cV.sig(v.drain())

            @blk.scalar
            def _(sc):
                cV.at(sc, b_v + 1)
                sc.activation(rs_ap, rs_ap, AF.Sqrt, bias=self.epsc[:, 0:1])
                cA.sig(sc.drain())

            @blk.vector
            def _(v):
                cA.wait(v)
                cSP.wait(v)
                v.reciprocal(rs_ap, rs_ap)
                v.drain()
                for et in range(ET):
                    v.tensor_tensor(tmp_ap, self.zt(et, sl), nm_ap, ALU.add)
                    v.drain()
                    v.tensor_tensor(self.zt(et, sl), tmp_ap, rs_ap, ALU.mult)
                cV.sig(v.drain())

        @blk.vector
        def _(v):
            cZ.sig(v.drain())

    def layer_ffn(self, sq, li):
        blk = self.blk
        t = "Fc"
        cA, cV, cT, cZ = self.ch(t + "a"), self.ch(t + "v"), self.ch(t + "t"), self.ch(t + "z")
        cWF = self.ch(t + "wf")
        aA, aV, aT = self.ch("Aca"), self.ch("Acv"), self.ch("Act")
        aOE = self.ch("Acoe")
        zcb, sqb = self.scb(8192, 9216), self.scb(9216, 10240)
        nm_ap, rs_ap = self.R34f[:, 0:1024], self.R34f[:, 1024:2048]
        tmp_ap = self.R34f[:, 2048:3072]
        gt_ap = self.scb(8192, 9216)

        for qh in range(QH):
            sl = slice(qh * SH, (qh + 1) * SH)
            b_a, b_t, b_v = cA.n, cT.n, cV.n

            if qh == 0:
                @blk.vector
                def _(v):
                    aV.wait(v)
                    aT.wait(v)
                    aA.wait(v)
                    aOE.wait(v)
                    for et in range(4, ET):
                        v.memset(self.xbt(et, slice(0, 1)), 0.0)
                    cV.sig(v.drain())
                b_v = cV.n

            @blk.scalar
            def _(sc):
                aV.wait(sc)
                aT.wait(sc)
                aOE.wait(sc)
                cV.wait(sc)
                for et in range(ET):
                    cT.at(sc, b_t + et)
                    dst = self.xbt(et, sl)
                    if et < 4:
                        sc.activation(dst, self.zt(et, sl), AF.Identity)
                    elif qh == 0:
                        sc.activation(dst[:, 1:SH], self.z[:, et * S: et * S + SH - 1], AF.Identity)
                    else:
                        sc.activation(dst, self.z[:, et * S + SH - 1: et * S + 2 * SH - 1], AF.Identity)
                    sc.activation(sqb, dst, AF.Square)
                    cA.sig(sc.drain(), 2)

            @blk.tensor
            def _(t_):
                for et in range(ET):
                    t_.wait_ge(cA.h, b_a + 2 * et + 2)
                    zc2 = self.xbt(et, sl)
                    t_.matmul(self.P_SC[:, 0:512], self.ones_bf[:], zc2[:, 0:512], start=(et == 0), stop=(et == ET - 1))
                    t_.matmul(self.P_SC[:, 512:1024], self.ones_bf[:], zc2[:, 512:1024], start=(et == 0), stop=(et == ET - 1))
                    t_.matmul(self.P_OUT[:, 0:512], self.ones_bf[:], sqb[:, 0:512], start=(et == 0), stop=(et == ET - 1))
                    t_.matmul(self.P_OUT[:, 512:1024], self.ones_bf[:], sqb[:, 512:1024], start=(et == 0), stop=(et == ET - 1))
                    cT.sig(t_.drain())

            @blk.vector
            def _(v):
                cT.at(v, b_t + ET)
                v.tensor_scalar(nm_ap, self.P_SC[:, 0:1024], -1.0 / E, None, ALU.mult)
                v.drain()
                v.tensor_tensor(rs_ap, nm_ap, nm_ap, ALU.mult)
                v.drain()
                v.scalar_tensor_tensor(rs_ap, self.P_OUT[:, 0:1024], 1.0 / E, rs_ap, ALU.mult, ALU.subtract)
                cV.sig(v.drain())

            @blk.scalar
            def _(sc):
                cV.at(sc, b_v + 1)
                sc.activation(rs_ap, rs_ap, AF.Sqrt, bias=self.epsc[:, 0:1])
                cA.sig(sc.drain())

            @blk.vector
            def _(v):
                cA.wait(v)
                v.reciprocal(rs_ap, rs_ap)
                cV.sig(v.drain())

            for et in range(ET):
                @blk.vector
                def _(v):
                    cV.wait(v)
                    cA.wait(v)
                    v.tensor_tensor(tmp_ap, self.xbt(et, sl), nm_ap, ALU.add)
                    v.drain()
                    v.tensor_tensor(tmp_ap, tmp_ap, rs_ap, ALU.mult)
                    cV.sig(v.drain())

                @blk.scalar
                def _(sc):
                    cV.wait(sc)
                    sc.activation(self.xbt(et, sl), tmp_ap, AF.Identity,
                                  scale=self.lb[:, et:et + 1], bias=self.lb[:, 8 + et:9 + et])
                    cA.sig(sc.drain())

        if self.upto == "l1zn":
            st_ = self.ch("store")

            @blk.gpsimd
            def _(g):
                cA.wait(g)
                cV.wait(g)
                for et in range(ET):
                    st_.sig(g.dma_start(self.outT[sq, et * 128:(et + 1) * 128, :], self.xbt(et)), 16)
                st_.wait(g)
            return

        for sc2 in range(2):
            ssl = slice(sc2 * 1024, sc2 * 1024 + 1024)
            b_wf, b_t, b_v, b_a = cWF.n, cT.n, cV.n, cA.n
            wu = self.ch(t + "wu")
            b_wu = wu.n

            @blk.gpsimd
            def _(g):
                cA.wait(g)
                cV.wait(g)
                cT.wait(g)
                for ct in range(CT):
                    w = min(128, CI - ct * 128)
                    if ct >= 2:
                        wu.at(g, b_wu + ct - 1)
                    cWF.sig(self.wdma_w(g, self.W1_d[li], ct * 128, (2 * ct % 4) * 2048, w), 16)
                    cWF.sig(self.wdma_w(g, self.W1_d[li], CI + ct * 128, ((2 * ct + 1) % 4) * 2048, w), 16)

            for ct in range(CT):
                w = min(128, CI - ct * 128)

                @blk.tensor
                def _(t_):
                    t_.wait_ge(cWF.h, b_wf + 32 * (ct + 1))
                    cV.at(t_, b_v + ct)
                    cA.at(t_, b_a + ct)
                    for part, ps in [(0, self.P_SC[:, 0:1024]), (1, self.P_OUT[:, 0:1024])]:
                        so = ((2 * ct + part) % 4) * 2048
                        for et in range(ET):
                            lhs = self.SCb[:, so + et * w: so + et * w + w]
                            t_.matmul(ps[0:w, 0:512], lhs, self.xbt(et, slice(ssl.start, ssl.start + 512)),
                                      start=(et == 0), stop=(et == ET - 1))
                            i_ = t_.matmul(ps[0:w, 512:1024], lhs, self.xbt(et, slice(ssl.start + 512, ssl.start + 1024)),
                                           start=(et == 0), stop=(et == ET - 1))
                    cT.sig(t_.drain())
                    wu.sig(t_.drain())

                @blk.scalar
                def _(sc):
                    cT.at(sc, b_t + ct + 1)
                    sc.activation(gt_ap[0:w, :], self.P_OUT[0:w, 0:1024], AF.Gelu,
                                  bias=self.lb[0:w, 46 + ct:47 + ct])
                    cA.sig(sc.drain())

                @blk.vector
                def _(v):
                    cA.at(v, b_a + ct + 1)
                    v.scalar_tensor_tensor(self.A48[0:w, ct * 1024:(ct + 1) * 1024], self.P_SC[0:w, 0:1024],
                                           self.lb[0:w, 24 + ct:25 + ct], gt_ap[0:w, :], ALU.add, ALU.mult)
                    cV.sig(v.drain())

            if self.upto == "l1h2" and sc2 == 0:
                st_ = self.ch("store")

                @blk.gpsimd
                def _(g):
                    cV.wait(g)
                    cA.wait(g)
                    for r in range(11):
                        dsti = self.outT[0, r * 128:(r + 1) * 128, 0:2048] if r < 8 else \
                               self.outT[1, (r - 8) * 128:(r - 7) * 128, 0:2048]
                        st_.sig(g.dma_start(dsti, self.A48[:, r * 2048:(r + 1) * 2048]), 16)
                    st_.wait(g)
                return

            NFG, NK4 = 2, 4
            for fg in range(NFG):
                b_wf2, b_t2, b_v2 = cWF.n, cT.n, cV.n
                w2u = self.ch(t + "w2u")
                b_w2u = w2u.n

                @blk.gpsimd
                def _(g):
                    cV.at(g, b_v2)
                    cT.wait(g)
                    for ct in range(CT):
                        w = min(128, CI - ct * 128)
                        if ct >= 4:
                            w2u.at(g, b_w2u + ct - 3)
                        dst = self.SCb[:, (ct % 4) * 1024: (ct % 4) * 1024 + 1024]
                        cWF.sig(g.dma_start(dst[0:w, :], self.W2_d[li, ct * 128: ct * 128 + w, :]), 16)

                ps4 = [self.P_SC[:, 0:1024], self.P_OUT[:, 0:1024], self.P_OUT[:, 1024:2048], None]

                @blk.tensor
                def _(t_):
                    for ct in range(CT):
                        w = min(128, CI - ct * 128)
                        t_.wait_ge(cWF.h, b_wf2 + 16 * (ct + 1))
                        if ct == 0:
                            cV.at(t_, b_v2)
                        slot = self.SCb[:, (ct % 4) * 1024: (ct % 4) * 1024 + 1024]
                        for k4 in range(NK4):
                            ft = fg * NK4 + k4
                            lhs = slot[0:w, ft * 128: ft * 128 + 128]
                            rhs0 = self.A48[0:w, ct * 1024 + 0: ct * 1024 + 512]
                            rhs1 = self.A48[0:w, ct * 1024 + 512: ct * 1024 + 1024]
                            if k4 < 3:
                                t_.matmul(ps4[k4][:, 0:512], lhs, rhs0, start=(ct == 0), stop=(ct == CT - 1))
                                i_ = t_.matmul(ps4[k4][:, 512:1024], lhs, rhs1, start=(ct == 0), stop=(ct == CT - 1))
                            else:
                                t_.matmul(self.P_OUT[:, 2048:2560], lhs, rhs0, start=(ct == 0), stop=(ct == CT - 1))
                                i_ = t_.matmul(self.P_Xf[:, 0:512], lhs, rhs1, start=(ct == 0), stop=(ct == CT - 1))
                        w2u.sig(t_.drain())
                    cT.sig(t_.drain())

                @blk.vector
                def _(v):
                    cT.at(v, b_t2 + 1)
                    for k4 in range(NK4):
                        ft = fg * NK4 + k4
                        bcol = self.lb[:, 16 + ft:17 + ft]
                        if k4 < 3:
                            v.scalar_tensor_tensor(self.zt(ft, ssl), ps4[k4][:, 0:1024], bcol,
                                                   self.zt(ft, ssl), ALU.add, ALU.add)
                        else:
                            v.scalar_tensor_tensor(self.zt(ft, slice(ssl.start, ssl.start + 512)),
                                                   self.P_OUT[:, 2048:2560], bcol,
                                                   self.zt(ft, slice(ssl.start, ssl.start + 512)), ALU.add, ALU.add)
                            v.scalar_tensor_tensor(self.zt(ft, slice(ssl.start + 512, ssl.start + 1024)),
                                                   self.P_Xf[:, 0:512], bcol,
                                                   self.zt(ft, slice(ssl.start + 512, ssl.start + 1024)), ALU.add, ALU.add)
                        v.drain()
                    cV.sig(v.drain())

        @blk.vector
        def _(v):
            cZ.sig(v.drain())

    def final_ln(self, sq):
        blk = self.blk
        t = "Fc"
        cA, cV, cT, cZ = self.ch(t + "a"), self.ch(t + "v"), self.ch(t + "t"), self.ch(t + "z")
        cX = self.ch(t + "x1")
        st_ = self.ch("store")
        zcb, sqb = self.scb(8192, 9216), self.scb(9216, 10240)
        nm_ap, rs_ap = self.scf(2048, 3072), self.scf(3072, 4096)
        x1b = lambda i: self.scf((i % 2) * 1024, (i % 2) * 1024 + 1024)
        for qh in range(QH):
            sl = slice(qh * SH, (qh + 1) * SH)
            b_a, b_t, b_v, b_x = cA.n, cT.n, cV.n, cX.n

            @blk.scalar
            def _(sc):
                cZ.wait(sc)
                cV.wait(sc)
                for et in range(ET):
                    cT.at(sc, b_t + et)
                    sc.activation(zcb, self.zt(et, sl), AF.Identity)
                    sc.activation(sqb, self.zt(et, sl), AF.Square)
                    cA.sig(sc.drain(), 2)

            @blk.tensor
            def _(t_):
                cZ.wait(t_)
                for et in range(ET):
                    t_.wait_ge(cA.h, b_a + 2 * et + 2)
                    t_.matmul(self.P_SC[:, 0:512], self.ones_bf[:], zcb[:, 0:512], start=(et == 0), stop=(et == ET - 1))
                    t_.matmul(self.P_SC[:, 512:1024], self.ones_bf[:], zcb[:, 512:1024], start=(et == 0), stop=(et == ET - 1))
                    t_.matmul(self.P_OUT[:, 0:512], self.ones_bf[:], sqb[:, 0:512], start=(et == 0), stop=(et == ET - 1))
                    t_.matmul(self.P_OUT[:, 512:1024], self.ones_bf[:], sqb[:, 512:1024], start=(et == 0), stop=(et == ET - 1))
                    cT.sig(t_.drain())

            @blk.vector
            def _(v):
                cT.at(v, b_t + ET)
                v.tensor_scalar(nm_ap, self.P_SC[:, 0:1024], -1.0 / E, None, ALU.mult)
                v.drain()
                v.tensor_tensor(rs_ap, nm_ap, nm_ap, ALU.mult)
                v.drain()
                v.scalar_tensor_tensor(rs_ap, self.P_OUT[:, 0:1024], 1.0 / E, rs_ap, ALU.mult, ALU.subtract)
                cV.sig(v.drain())

            @blk.scalar
            def _(sc):
                cV.at(sc, b_v + 1)
                sc.activation(rs_ap, rs_ap, AF.Sqrt, bias=self.epsc[:, 0:1])
                cA.sig(sc.drain())

            @blk.vector
            def _(v):
                cA.wait(v)
                v.reciprocal(rs_ap, rs_ap)
                cV.sig(v.drain())
            b_v2 = cV.n

            ox = self.ch(t + "ox")
            b_ox, b_st = ox.n, st_.n
            for et in range(ET):
                @blk.sync
                def _(s):
                    if et < 2:
                        cV.at(s, b_v2)
                    else:
                        st_.at(s, b_st + 16 * (et - 1))
                    cX.sig(s.dma_start(x1b(et), self.x1sp[sq, et * 128:(et + 1) * 128, sl]), 16)

                @blk.vector
                def _(v):
                    cX.at(v, b_x + 16 * (et + 1))
                    tmp2 = self.R34f[:, 2048:3072]
                    v.tensor_tensor(tmp2, self.zt(et, sl), nm_ap, ALU.add)
                    v.drain()
                    v.tensor_tensor(tmp2, tmp2, rs_ap, ALU.mult)
                    v.drain()
                    v.tensor_tensor(x1b(et), x1b(et), tmp2, ALU.add)
                    ox.sig(v.drain())

                @blk.sync
                def _(s):
                    ox.at(s, b_ox + et + 1)
                    st_.sig(s.dma_start(self.outT[sq, et * 128:(et + 1) * 128, sl], x1b(et)), 16)

        @blk.sync
        def _(s):
            st_.wait(s)

    def _wo_ln_res(self, sq):
        blk = self.blk
        gW, mm, mmS, ev = self.ch("oGW"), self.ch("oGM"), self.ch("oGT"), self.ch("oGE")
        stt, wZ = self.ch("oGS"), self.ch("oGZ")
        a0ev, a1ev, a1sm = self.ch("ath0ev"), self.ch("ath1ev"), self.ch("ath1sm")
        a0pv, a1pv = self.ch("ath0pv"), self.ch("ath1pv")
        zcb, sqb = self.scb(8192, 9216), self.scb(9216, 10240)
        nm_ap, rs_ap = self.R34f[:, 0:1024], self.R34f[:, 1024:2048]
        tmpf, tmpf2 = self.R34f[:, 2048:3072], self.R34f[:, 3072:4096]
        b_gW = gW.n

        @blk.gpsimd
        def _(g):
            a0ev.wait(g)
            a1ev.wait(g)
            a1sm.wait(g)
            a0pv.wait(g)
            a1pv.wait(g)
            gW.sig(self.wdma(g, self.Wd["WoT"], 0, 0), 16)
            gW.sig(self.wdma(g, self.Wd["WoT"], 512, 1), 16)

        for qh in range(QH):
            sl = slice(qh * SH, (qh + 1) * SH)
            b_mm, b_mmS, b_ev, b_stt, b_wZ = mm.n, mmS.n, ev.n, stt.n, wZ.n
            if self.upto == "wop1" and qh == 1:
                st_ = self.ch("store")

                @blk.sync
                def _(s):
                    stt.at(s, b_stt)
                    st_.sig(s.dma_start(self.outT[0, 0:128, 0:1024], nm_ap), 16)
                    st_.sig(s.dma_start(self.outT[0, 128:256, 0:1024], rs_ap), 16)
                    st_.sig(s.dma_start(self.outT[0, 256:384, 0:1024], self.SCf[:, 0:1024]), 16)
                    st_.wait(s)
                return

            @blk.tensor
            def _(t):
                t.wait_ge(gW.h, b_gW + 32)
                if qh == 1:
                    wZ.at(t, b_wZ)
                else:
                    a1ev.wait(t)
                for ft in range(ET):
                    ev.at(t, b_ev + ft)
                    for et in range(ET):
                        wch, col = (0, ft * 128) if ft < 4 else (1, ft * 128 - 512)
                        lhs = self.wtile(wch, et, col)
                        t.matmul(self.P_SC[:, 0:512], lhs, self.xbt(et, slice(sl.start, sl.start + 512)),
                                 start=(et == 0), stop=(et == ET - 1))
                        i_ = t.matmul(self.P_SC[:, 512:1024], lhs, self.xbt(et, slice(sl.start + 512, sl.start + 1024)),
                                      start=(et == 0), stop=(et == ET - 1))
                    mm.sig(t.drain())
                    ev.at(t, b_ev + ft + 1)
                    t.matmul(self.P_OUT[:, 1024:1536], self.ones_bf[:], zcb[:, 0:512], start=(ft == 0), stop=(ft == ET - 1))
                    t.matmul(self.P_OUT[:, 1536:2048], self.ones_bf[:], zcb[:, 512:1024], start=(ft == 0), stop=(ft == ET - 1))
                    t.matmul(self.P_OUT[:, 2048:2560], self.ones_bf[:], sqb[:, 0:512], start=(ft == 0), stop=(ft == ET - 1))
                    t.matmul(self.P_Xf[:, 0:512], self.ones_bf[:], sqb[:, 512:1024], start=(ft == 0), stop=(ft == ET - 1))
                    mmS.sig(t.drain())

            @blk.scalar
            def _(sc):
                for ft in range(ET):
                    mm.at(sc, b_mm + ft + 1)
                    mmS.at(sc, b_mmS + ft)
                    bcol = self.bias4[:, 2 * ET + ft:2 * ET + ft + 1]
                    sc.activation(zcb, self.P_SC[:, 0:1024], AF.Identity, bias=bcol)
                    sc.activation(sqb, self.P_SC[:, 0:1024], AF.Square, bias=bcol)
                    ev.sig(sc.drain())

            @blk.vector
            def _(v):
                mmS.at(v, b_mmS + ET)
                v.tensor_scalar(nm_ap, self.P_OUT[:, 1024:2048], -1.0 / E, None, ALU.mult)
                v.drain()
                v.tensor_tensor(rs_ap, nm_ap, nm_ap, ALU.mult)
                v.drain()
                v.scalar_tensor_tensor(rs_ap[:, 0:512], self.P_OUT[:, 2048:2560], 1.0 / E,
                                       rs_ap[:, 0:512], ALU.mult, ALU.subtract)
                v.scalar_tensor_tensor(rs_ap[:, 512:1024], self.P_Xf[:, 0:512], 1.0 / E,
                                       rs_ap[:, 512:1024], ALU.mult, ALU.subtract)
                stt.sig(v.drain())

            @blk.scalar
            def _(sc):
                stt.at(sc, b_stt + 1)
                sc.activation(rs_ap, rs_ap, AF.Sqrt, bias=self.epsc[:, 0:1])
                stt.sig(sc.drain())

            @blk.vector
            def _(v):
                stt.at(v, b_stt + 2)
                v.reciprocal(rs_ap, rs_ap)
                stt.sig(v.drain())

            b_mm2 = mm.n

            @blk.tensor
            def _(t):
                stt.at(t, b_stt + 3)
                for ft in range(ET):
                    wZ.at(t, b_wZ + ft)
                    for et in range(ET):
                        wch, col = (0, ft * 128) if ft < 4 else (1, ft * 128 - 512)
                        lhs = self.wtile(wch, et, col)
                        t.matmul(self.P_SC[:, 0:512], lhs, self.xbt(et, slice(sl.start, sl.start + 512)),
                                 start=(et == 0), stop=(et == ET - 1))
                        i_ = t.matmul(self.P_SC[:, 512:1024], lhs, self.xbt(et, slice(sl.start + 512, sl.start + 1024)),
                                      start=(et == 0), stop=(et == ET - 1))
                    mm.sig(t.drain())

            @blk.vector
            def _(v):
                for ft in range(ET):
                    mm.at(v, b_mm2 + ft + 1)
                    v.scalar_tensor_tensor(tmpf, self.P_SC[:, 0:1024], self.bias4[:, 2 * ET + ft:2 * ET + ft + 1],
                                           nm_ap, ALU.add, ALU.add)
                    v.drain()
                    v.tensor_tensor(tmpf2, tmpf, rs_ap, ALU.mult)
                    v.drain()
                    v.tensor_tensor(self.zt(ft, sl), self.zt(ft, sl), tmpf2, ALU.add)
                    wZ.sig(v.drain())


def build_program(upto=None):
    kb = KB(upto or UPTO)
    partial = (upto or UPTO) not in ("stageG", "full") and not (upto or UPTO).startswith("layer")
    partial = partial or (upto or UPTO) in ("ln3z", "l1attn", "l1ao", "l1rg", "l1z2", "l1zn", "l1h2")
    with kb.stack:
        kb.alloc()
        kb.init_consts()
        for sq in range(1 if partial else 2):
            ld = kb.load_x(sq)
            waits = [ld] if sq == 0 else [ld, "oGZ"]
            ln0 = kb.ln_z_to_xb("l0s", waits)
            kb.stage_g(sq, ln0)
            if kb.upto in ("lsc", "ln0", "kproj", "vproj", "qproj", "attn", "wop1", "h14"):
                continue
            if kb.upto == "stageG":
                kb.store_z(sq, ["oGZ"])
                continue
            kb.spill_ln3(sq)
            if kb.upto == "ln3z":
                kb.dump_z(sq, ["Fcv", "Fcz"])
                continue
            nlayers = NL if kb.upto in ("full", "l1attn", "l1ao", "l1rg", "l1z2", "l1zn", "l1h2") else int(kb.upto[5:])
            for li in range(nlayers):
                kb.layer_attn(sq, li)
                if kb.upto in ("l1ao", "l1rg"):
                    break
                if kb.upto == "l1attn":
                    kb.dump_z(sq, ["Acv", "Aca", "Act", "Acoe"])
                    break
                kb.layer_ffn(sq, li)
                if kb.upto in ("l1z2", "l1zn", "l1h2"):
                    if kb.upto == "l1z2":
                        kb.dump_z(sq, ["Fcz", "Fcv", "Fca"])
                    break
            if kb.upto in ("l1attn", "l1ao", "l1rg", "l1z2", "l1zn", "l1h2"):
                continue
            kb.final_ln(sq)
    return kb.nc


def kernel(**inputs):
    d = _prep(inputs)
    nc = build_program()
    shared = {k: v for k, v in d.items() if not k.startswith("_")}
    in_maps = [dict(shared, xT=d["_x"][c]) for c in range(8)]
    res = run_bass_kernel_spmd(nc, in_maps, core_ids=list(range(8)))
    out = np.empty((16, S, E), np.float32)
    for c in range(8):
        out[2 * c:2 * c + 2] = res.results[c]["outT"].transpose(0, 2, 1)
    return out


# version 02
# speedup vs baseline: 1.0254x; 1.0254x over previous
import numpy as np
import ml_dtypes
from contextlib import ExitStack
import concourse.bass as bass
import concourse.mybir as mybir
from concourse.bass_utils import run_bass_kernel_spmd

F32 = mybir.dt.float32
BF16 = mybir.dt.bfloat16
AF = mybir.ActivationFunctionType
ALU = mybir.AluOpType
bf16 = ml_dtypes.bfloat16

E, S, H, D = 1024, 2048, 16, 64
ET, ST = 8, 16
NL = 6
C1N, CI, CT = 5460, 2730, 22
QH, SH = 2, 1024
EPS = 1e-5
LN8 = float(np.log(8.0))
UPTO = "full"


def _rotary_tables():
    half = np.arange(0, D, 2, dtype=np.float64)
    inv_freq = 1.0 / 10000.0 ** (half / D)
    t = np.arange(S, dtype=np.float64)
    freqs = np.concatenate([t[:, None] * inv_freq[None, :]] * 2, -1)
    base_scale = (half + 0.4 * D) / (1.4 * D)
    power = (t - S // 2) / 128.0
    scale = np.concatenate([base_scale[None, :] ** power[:, None]] * 2, -1)
    cos, sin = np.cos(freqs), np.sin(freqs)

    def sg(tb):
        o = tb.copy()
        o[:, :32] = -o[:, :32]
        return o
    return ((cos * scale).astype(np.float32), sg(sin * scale).astype(np.float32),
            (cos / scale).astype(np.float32), sg(sin / scale).astype(np.float32))


def _prep(inp):
    d = {}
    x = np.asarray(inp["L0_attn_x"], np.float32)
    d["_x"] = [np.ascontiguousarray(x[2 * c:2 * c + 2].transpose(0, 2, 1)) for c in range(8)]
    for nm in ["Wq", "Wk", "Wv", "Wo"]:
        d[nm + "T"] = np.ascontiguousarray(np.asarray(inp[f"L0_attn_{nm}"], np.float32).T)
    for nm in ["bq", "bk", "bo"]:
        d[nm] = np.ascontiguousarray(np.asarray(inp[f"L0_attn_{nm}"], np.float32).reshape(ET, 128).T)
    d["bv_bc"] = np.ascontiguousarray(np.broadcast_to(np.asarray(inp["L0_attn_bv"], np.float32)[None, :], (128, E)))
    d["dp_bc"] = np.ascontiguousarray(np.broadcast_to(np.asarray(inp["L0_attn_decay_params"], np.float32).reshape(1, 2 * H), (128, 2 * H)))
    mwf = np.asarray(inp["L0_attn_mix_weights"], np.float32)
    d["mw_bc"] = np.ascontiguousarray(np.broadcast_to(mwf.reshape(1, 2 * H), (128, 2 * H)))
    d["mws_bc"] = np.ascontiguousarray(np.broadcast_to(mwf[:, ::-1] if False else mwf[:, [1, 0]].reshape(1, 2 * H), (128, 2 * H)))
    d["Wqkv"] = np.asarray(inp["L0_ffn_Wqkv"], np.float32)
    d["Wof"] = np.asarray(inp["L0_ffn_Wo"], np.float32)
    d["W1"] = np.asarray(inp["L0_ffn_W1"], np.float32)
    d["W2"] = np.asarray(inp["L0_ffn_W2"], np.float32)
    d["lng"] = np.ascontiguousarray(np.asarray(inp["L0_ffn_ln_g"], np.float32).reshape(NL, ET, 128).transpose(0, 2, 1))
    d["lnb"] = np.ascontiguousarray(np.asarray(inp["L0_ffn_ln_b"], np.float32).reshape(NL, ET, 128).transpose(0, 2, 1))
    b1 = np.asarray(inp["L0_ffn_b1"], np.float32)
    pa = np.zeros((NL, CT * 128), np.float32)
    pg = np.zeros((NL, CT * 128), np.float32)
    pa[:, :CI] = b1[:, :CI]
    pg[:, :CI] = b1[:, CI:]
    d["b1a"] = np.ascontiguousarray(pa.reshape(NL, CT, 128).transpose(0, 2, 1))
    d["b1g"] = np.ascontiguousarray(pg.reshape(NL, CT, 128).transpose(0, 2, 1))
    d["b2"] = np.ascontiguousarray(np.asarray(inp["L0_ffn_b2"], np.float32).reshape(NL, ET, 128).transpose(0, 2, 1))
    qsf = np.asarray(inp["L0_ffn_q_scale"], np.float32)
    ksf = np.asarray(inp["L0_ffn_k_scale"], np.float32)
    d["qs_bc"] = np.ascontiguousarray(np.broadcast_to(qsf[:, None, :], (NL, 128, D)))
    d["ks_bc"] = np.ascontiguousarray(np.broadcast_to(ksf[:, None, :], (NL, 128, D)))
    d["qsr_bc"] = np.ascontiguousarray(np.broadcast_to(np.roll(qsf, -32, -1)[:, None, :], (NL, 128, D)))
    d["ksr_bc"] = np.ascontiguousarray(np.broadcast_to(np.roll(ksf, -32, -1)[:, None, :], (NL, 128, D)))
    d["tcq"], d["tsq"], d["tck"], d["tsk"] = _rotary_tables()
    i = np.arange(128)
    d["tlo"] = np.ascontiguousarray((i[:, None] >= i[None, :]).astype(bf16))
    d["thi"] = np.ascontiguousarray((i[:, None] <= i[None, :]).astype(bf16))
    return d


class Chan:
    def __init__(self, nc, stack, name):
        self.h = stack.enter_context(nc.semaphore(name))
        self.n = 0

    def sig(self, instr, k=1):
        instr.then_inc(self.h, k)
        self.n += k
        return instr

    def wait(self, eng):
        eng.wait_ge(self.h, self.n)

    def at(self, eng, n):
        if n > 0:
            eng.wait_ge(self.h, n)


class KB:
    def __init__(self, upto):
        self.upto = upto
        nc = self.nc = bass.Bass("TRN2", target_bir_lowering=False)
        self.stack = ExitStack()
        di = lambda nm, sh, dt=F32: nc.dram_tensor(nm, sh, dt, kind="ExternalInput").ap()
        self.xT = di("xT", [2, E, S])
        self.Wd = {nm: di(nm, [E, E]) for nm in ["WqT", "WkT", "WvT", "WoT"]}
        self.Bd = {nm: di(nm, [128, ET]) for nm in ["bq", "bk", "bo"]}
        self.bv_d = di("bv_bc", [128, E])
        self.dp_d = di("dp_bc", [128, 2 * H])
        self.mw_d = di("mw_bc", [128, 2 * H])
        self.mws_d = di("mws_bc", [128, 2 * H])
        self.Wqkv_d = di("Wqkv", [NL, E, 3 * E])
        self.Wof_d = di("Wof", [NL, E, E])
        self.W1_d = di("W1", [NL, E, C1N])
        self.W2_d = di("W2", [NL, CI, E])
        self.lng_d = di("lng", [NL, 128, ET])
        self.lnb_d = di("lnb", [NL, 128, ET])
        self.b1a_d = di("b1a", [NL, 128, CT])
        self.b1g_d = di("b1g", [NL, 128, CT])
        self.b2_d = di("b2", [NL, 128, ET])
        self.qs_d = di("qs_bc", [NL, 128, D])
        self.ks_d = di("ks_bc", [NL, 128, D])
        self.qsr_d = di("qsr_bc", [NL, 128, D])
        self.ksr_d = di("ksr_bc", [NL, 128, D])
        self.Td = {nm: di(nm, [S, D]) for nm in ["tcq", "tsq", "tck", "tsk"]}
        self.tlo_d = di("tlo", [128, 128], BF16)
        self.thi_d = di("thi", [128, 128], BF16)
        self.outT = nc.dram_tensor("outT", [2, E, S], F32, kind="ExternalOutput").ap()
        self.x1sp = nc.dram_tensor("x1sp", [2, E, S], F32, kind="Internal").ap()
        self.chans = {}

    def sb(self, nm, sh, dt):
        return self.stack.enter_context(self.nc.sbuf_tensor(nm, sh, dt))

    def ch(self, nm):
        if nm not in self.chans:
            self.chans[nm] = Chan(self.nc, self.stack, nm)
        return self.chans[nm]

    def alloc(self):
        st = self.stack
        self.z = self.sb("z", [128, ET * S], F32)
        self.A48 = self.sb("A48", [128, 24576], BF16)
        self.XB = self.sb("XB", [128, 16384], BF16)
        self.R34 = self.sb("R34", [128, 16640], BF16)
        self.R34f = None
        self.SCf = self.sb("SCf", [128, 5120], F32)
        self.SCb = self.SCf[:, :].bitcast(BF16)
        self.R34f = self.R34[:, :].bitcast(F32)
        self.ones_bf = self.sb("ones_bf", [128, 128], BF16)
        self.idn = self.sb("idn", [128, 128], BF16)
        self.tri = self.sb("tri", [128, 256], BF16)
        self.bias4 = self.sb("bias4", [128, 3 * ET], F32)
        self.lsc = self.sb("lsc", [128, 64], F32)
        self.anat = self.sb("anat", [128, 512], BF16)
        self.rnrm = self.sb("rnrm", [128, 64], F32)
        self.epsc = self.sb("epsc", [128, 1], F32)
        self.mws = self.sb("mws", [128, 32], F32)
        self.lb = self.sb("lb", [128, 70], F32)
        self.ctab = self.sb("ctab", [128, 16 * 256], BF16)
        self.P_SC = st.enter_context(self.nc.psum_tensor("P_SC", [128, 1024], F32))
        self.P_OUT = st.enter_context(self.nc.psum_tensor("P_OUT", [128, 2560], F32))
        self.P_TR = st.enter_context(self.nc.psum_tensor("P_TR", [128, 1024], BF16))
        self.P_Xf = self.P_TR[:, :].bitcast(F32)
        self.blk = self.stack.enter_context(self.nc.Block())

    def scf(self, a, b):
        return self.SCf[:, a:b]

    def scb(self, a, b):
        return self.SCb[:, a:b]

    def zt(self, et, sl=None):
        return self.z[:, et * S:(et + 1) * S] if sl is None else self.z[:, et * S + sl.start:et * S + sl.stop]

    def xbt(self, et, sl=None):
        return self.XB[:, et * S:(et + 1) * S] if sl is None else self.XB[:, et * S + sl.start:et * S + sl.stop]

    def out_slot(self, idx):
        bank, off = divmod(idx, 7)
        o = bank * 512 + off * 65
        return self.P_OUT[:, o:o + 65]

    def psum4(self, parity):
        if parity == 0:
            return [self.P_SC[:, 0:512], self.P_SC[:, 512:1024],
                    self.P_OUT[:, 0:512], self.P_OUT[:, 512:1024]]
        return [self.P_OUT[:, 1024:1536], self.P_OUT[:, 1536:2048],
                self.P_OUT[:, 2048:2560], self.P_Xf[:, 0:512]]

    def wdma(self, g, dram, c0, half):
        dst = self.scb(half * 4096, half * 4096 + 4096).rearrange("p (a f) -> p a f", a=ET)
        src = dram[:, c0:c0 + 512].rearrange("(a p) f -> p a f", p=128)
        return g.dma_start(dst, src)

    def wtile(self, half, et, col):
        o = half * 4096 + et * 512 + col
        return self.SCb[:, o:o + 128]

    def init_consts(self):
        blk = self.blk
        init, mix = self.ch("init"), self.ch("mix")

        @blk.vector
        def _(v):
            v.memset(self.epsc[:], EPS)
            v.memset(self.ones_bf[:], 1.0)
            init.sig(v.drain())

        @blk.gpsimd
        def _(g):
            g.iota(self.idn[:], pattern=[[-1, 128]], base=0, channel_multiplier=1,
                   allow_small_or_imprecise_dtypes=True)
            init.sig(g.drain())

        @blk.sync
        def _(s):
            init.sig(s.dma_start(self.tri[:, 0:128], self.tlo_d[:, :]), 16)
            init.sig(s.dma_start(self.tri[:, 128:256], self.thi_d[:, :]), 16)
            init.sig(s.dma_start(self.bias4[:, 0:ET], self.Bd["bq"][:, :]), 16)
            init.sig(s.dma_start(self.bias4[:, ET:2 * ET], self.Bd["bk"][:, :]), 16)
            init.sig(s.dma_start(self.bias4[:, 2 * ET:3 * ET], self.Bd["bo"][:, :]), 16)
            init.sig(s.dma_start(self.rnrm[:, 0:32], self.dp_d[:, :]), 16)
            init.sig(s.dma_start(self.rnrm[:, 32:64], self.mw_d[:, :]), 16)
            init.sig(s.dma_start(self.mws[:, :], self.mws_d[:, :]), 16)

        @blk.vector
        def _(v):
            init.wait(v)
            v.tensor_scalar(self.idn[:], self.idn[:], 0.0, None, ALU.is_equal)
            v.tensor_scalar(self.lsc[:, 0:32], self.rnrm[:, 0:32], 0.01, 0.9999, ALU.max, ALU.min)
            init.sig(v.drain())

        @blk.scalar
        def _(sc):
            init.wait(sc)
            sc.activation(self.lsc[:, 0:32], self.lsc[:, 0:32], AF.Ln)
            sc.activation(self.lsc[:, 32:64], self.rnrm[:, 32:64], AF.Exp)
            sc.activation(self.mws[:, :], self.mws[:, :], AF.Exp)
            mix.sig(sc.drain())

        @blk.vector
        def _(v):
            mix.wait(v)
            v.tensor_tensor(self.rnrm[:, 0:32], self.lsc[:, 32:64], self.mws[:, :], ALU.add)
            mix.sig(v.drain())

        @blk.scalar
        def _(sc):
            mix.wait(sc)
            sc.activation(self.rnrm[:, 0:32], self.rnrm[:, 0:32], AF.Ln)
            mix.sig(sc.drain())

        @blk.vector
        def _(v):
            mix.wait(v)
            v.tensor_sub(self.lsc[:, 32:64], self.rnrm[:, 32:64], self.rnrm[:, 0:32])
            v.drain()
            v.tensor_scalar(self.lsc[:, 32:64], self.lsc[:, 32:64], LN8, None, ALU.subtract)
            mix.sig(v.drain())


    def load_x(self, sq):
        blk = self.blk
        ld = self.ch("ldx")

        @blk.sync
        def _(s):
            if sq == 1:
                self.ch("store").wait(s)
            for et in range(ET):
                ld.sig(s.dma_start(self.zt(et), self.xT[sq, et * 128:(et + 1) * 128, :]), 16)
        return ld

    def dump_z(self, sq, wait_list):
        blk = self.blk
        wait_list = [w if isinstance(w, Chan) else self.ch(w) for w in wait_list]
        st = self.ch("store")

        @blk.sync
        def _(s):
            for c in wait_list:
                c.wait(s)
            for et in range(ET):
                st.sig(s.dma_start(self.outT[sq, et * 128:(et + 1) * 128, :], self.zt(et)), 16)
            st.wait(s)

    def store_z(self, sq, wait_list):
        blk = self.blk
        wait_list = [w if isinstance(w, Chan) else self.ch(w) for w in wait_list]
        st = self.ch("store")

        @blk.sync
        def _(s):
            for c in wait_list:
                c.wait(s)
            for et in range(ET):
                st.sig(s.dma_start(self.outT[sq, et * 128:(et + 1) * 128, :], self.zt(et)), 16)
            st.wait(s)

    def ln_z_to_xb(self, tag, wait_chs):
        blk = self.blk
        wait_chs = [w if isinstance(w, Chan) else self.ch(w) for w in wait_chs]
        cc, cs, cv = self.ch(tag + "c"), self.ch(tag + "s"), self.ch(tag + "v")
        zc, sqb = self.scb(2048, 3072), self.scb(3072, 4096)
        nm_ap, rs_ap, tmp_ap = self.scf(2048, 3072), self.scf(3072, 4096), self.scf(0, 1024)
        for qh in range(QH):
            sl = slice(qh * SH, (qh + 1) * SH)
            b_cc, b_cs, b_cv = cc.n, cs.n, cv.n

            @blk.scalar
            def _(sc):
                for w in wait_chs:
                    w.wait(sc)
                cv.at(sc, b_cv)
                for et in range(ET):
                    cs.at(sc, b_cs + et)
                    sc.activation(zc, self.zt(et, sl), AF.Identity)
                    sc.activation(sqb, self.zt(et, sl), AF.Square)
                    cc.sig(sc.drain(), 2)

            @blk.tensor
            def _(t):
                for et in range(ET):
                    t.wait_ge(cc.h, b_cc + 2 * et + 2)
                    t.matmul(self.P_SC[:, 0:512], self.ones_bf[:], zc[:, 0:512], start=(et == 0), stop=(et == ET - 1))
                    t.matmul(self.P_SC[:, 512:1024], self.ones_bf[:], zc[:, 512:1024], start=(et == 0), stop=(et == ET - 1))
                    t.matmul(self.P_OUT[:, 0:512], self.ones_bf[:], sqb[:, 0:512], start=(et == 0), stop=(et == ET - 1))
                    t.matmul(self.P_OUT[:, 512:1024], self.ones_bf[:], sqb[:, 512:1024], start=(et == 0), stop=(et == ET - 1))
                    cs.sig(t.drain())

            @blk.vector
            def _(v):
                cs.at(v, b_cs + ET)
                v.tensor_scalar(nm_ap, self.P_SC[:, 0:1024], -1.0 / E, None, ALU.mult)
                v.drain()
                v.tensor_tensor(rs_ap, nm_ap, nm_ap, ALU.mult)
                v.drain()
                v.scalar_tensor_tensor(rs_ap, self.P_OUT[:, 0:1024], 1.0 / E, rs_ap, ALU.mult, ALU.subtract)
                cv.sig(v.drain())

            @blk.scalar
            def _(sc):
                cv.at(sc, b_cv + 1)
                sc.activation(rs_ap, rs_ap, AF.Sqrt, bias=self.epsc[:, 0:1])
                cv.sig(sc.drain())

            @blk.vector
            def _(v):
                cv.at(v, b_cv + 2)
                v.reciprocal(rs_ap, rs_ap)
                v.drain()
                for et in range(ET):
                    v.tensor_tensor(tmp_ap, self.zt(et, sl), nm_ap, ALU.add)
                    v.drain()
                    v.tensor_tensor(self.xbt(et, sl), tmp_ap, rs_ap, ALU.mult)
                cv.sig(v.drain())
        return cv

    def stage_g(self, sq, ln0):
        blk = self.blk
        if self.upto == "lsc":
            blk2 = self.blk
            stc = self.ch("store")

            @blk2.sync
            def _(s):
                self.ch("mix").wait(s)
                stc.sig(s.dma_start(self.outT[0, 0:128, 0:64], self.lsc[:, :]), 16)
                stc.wait(s)
            return
        if self.upto == "ln0":
            self.dump_bf16(sq, lambda et: self.xbt(et), [ln0])
            return
        self._proj_T("kP", "WkT", bias_col=1, dst_base=0, dst_w=2048, sl=slice(0, S),
                     pre_wait=[ln0], extra_dma=("bv",))
        if self.upto == "kproj":
            self.dump_bf16(sq, lambda et: self.A48[:, et * 2048:(et + 1) * 2048], ["kPE"])
            return
        self._v_proj(sq)
        if self.upto == "vproj":
            self.dump_bf16(sq, lambda et: self.R34[:, et * 2080:et * 2080 + 2048], ["vE"])
            return
        for qh in range(QH):
            self._proj_T(f"qh{qh}", "WqT", bias_col=0, dst_base=16384, dst_w=1024,
                         sl=slice(qh * SH, (qh + 1) * SH),
                         pre_wait=["vM", "vE"] if qh == 0
                         else ["ath0sm", "ath0pv", "ath0ts", "qh0M"])
            if self.upto == "qproj" and qh == 0:
                self.dump_bf16(sq, lambda et: self.A48[:, 16384 + et * 1024: 16384 + (et + 1) * 1024],
                               ["qh0E"], width=1024)
                return
            self._attn_half(sq, qh)
            if self.upto == "h14":
                return
        if self.upto == "attn":
            self.dump_bf16(sq, lambda et: self.xbt(et), ["ath0ev", "ath1ev"])
            return
        self._wo_ln_res(sq)

    def _proj_T(self, tag, wnm, bias_col, dst_base, dst_w, sl, pre_wait, extra_dma=()):
        blk = self.blk
        pre_wait = [c if isinstance(c, Chan) else self.ch(c) for c in pre_wait]
        gW, mm, ev = self.ch(tag + "W"), self.ch(tag + "M"), self.ch(tag + "E")
        nsc = dst_w // 512
        b_gW, b_mm, b_ev = gW.n, mm.n, ev.n

        @blk.gpsimd
        def _(g):
            for c in pre_wait:
                c.wait(g)
            gW.sig(self.wdma(g, self.Wd[wnm], 0, 0), 16)
            gW.sig(self.wdma(g, self.Wd[wnm], 512, 1), 16)
            if "bv" in extra_dma:
                gW.sig(g.dma_start(self.scf(4096, 5120), self.bv_d[:, :]), 16)

        @blk.tensor
        def _(t):
            t.wait_ge(gW.h, b_gW + 32)
            for c in pre_wait:
                c.wait(t)
            for ft in range(ET):
                ps4 = self.psum4(ft % 2)[0:nsc]
                ev.at(t, b_ev + ft - 1)
                for et in range(ET):
                    wch, col = (0, ft * 128) if ft < 4 else (1, ft * 128 - 512)
                    lhs = self.wtile(wch, et, col)
                    for j in range(nsc):
                        i_ = t.matmul(ps4[j], lhs, self.xbt(et, slice(sl.start + j * 512, sl.start + j * 512 + 512)),
                                      start=(et == 0), stop=(et == ET - 1))
                mm.sig(t.drain())

        @blk.scalar
        def _(sc):
            for ft in range(ET):
                ps4 = self.psum4(ft % 2)[0:nsc]
                mm.at(sc, b_mm + ft + 1)
                dst = self.A48[:, dst_base + ft * dst_w: dst_base + ft * dst_w + dst_w]
                for j in range(nsc):
                    sc.activation(dst[:, j * 512:j * 512 + 512], ps4[j], AF.Identity,
                                  bias=self.bias4[:, bias_col * ET + ft:bias_col * ET + ft + 1])
                ev.sig(sc.drain())

    def _v_proj(self, sq):
        blk = self.blk
        gW, mm, ev = self.ch("vW"), self.ch("vM"), self.ch("vE")
        kdone, kev, kw = self.ch("kPM"), self.ch("kPE"), self.ch("kPW")
        b_gW, b_mm, b_ev = gW.n, mm.n, ev.n

        @blk.gpsimd
        def _(g):
            kdone.wait(g)
            gW.sig(self.wdma(g, self.Wd["WvT"], 0, 0), 16)
            gW.sig(self.wdma(g, self.Wd["WvT"], 512, 1), 16)

        @blk.vector
        def _(v):
            va = self.R34[:, 0:16640].rearrange("p (st h c) -> p st h c", st=ST, h=H, c=65)
            v.memset(va[:, :, :, 64:65], 1.0)
            ev.sig(v.drain())
        b_ev += 1

        @blk.tensor
        def _(t):
            t.wait_ge(gW.h, b_gW + 32)
            kev.wait(t)
            for st in range(ST):
                psA = self.P_SC[:, 0:1024] if st % 2 == 0 else self.P_OUT[:, 0:1024]
                ev.at(t, b_ev + st - 1)
                for et in range(ET):
                    lhs = self.XB[:, et * 2048 + st * 128: et * 2048 + st * 128 + 128]
                    t.matmul(psA[:, 0:512], lhs, self.scb(0, 4096)[:, et * 512:et * 512 + 512],
                             start=(et == 0), stop=(et == ET - 1))
                    i_ = t.matmul(psA[:, 512:1024], lhs, self.scb(4096, 8192)[:, et * 512:et * 512 + 512],
                                  start=(et == 0), stop=(et == ET - 1))
                mm.sig(t.drain())

        @blk.vector
        def _(v):
            kw.at(v, 48)
            bv = self.scf(4096, 5120).rearrange("p (h c) -> p h c", h=H, c=64)
            for st in range(ST):
                psA = self.P_SC[:, 0:1024] if st % 2 == 0 else self.P_OUT[:, 0:1024]
                mm.at(v, b_mm + st + 1)
                dst = self.R34[:, st * 1040: st * 1040 + 1040].rearrange("p (h c) -> p h c", h=H, c=65)[:, :, 0:64]
                v.tensor_tensor(dst, psA.rearrange("p (h c) -> p h c", h=H, c=64), bv, ALU.add)
                ev.sig(v.drain())

    def _attn_half(self, sq, qh):
        blk = self.blk
        tag = f"ath{qh}"
        aD = self.ch(tag + "D")
        aM = self.ch(tag + "M")
        aE1 = self.ch(tag + "e")
        aQK = self.ch(tag + "qk")
        aSM = self.ch(tag + "sm")
        aPT = self.ch(tag + "pt")
        aPV = self.ch(tag + "pv")
        aTS = self.ch(tag + "ts")
        aTR = self.ch(tag + "tr")
        aEV = self.ch(tag + "ev")
        aIO = self.ch(tag + "io")
        Dab, M, Mt, SM = self.scf(0, 1024), self.scf(1024, 2048), self.scf(2048, 3072), self.scf(3072, 4096)
        PT = lambda par: self.scb(8192 + par * 1024, 8192 + par * 1024 + 1024)
        qdone, vdone = self.ch(f"qh{qh}E"), self.ch("vE")
        bD, bM, bE1, bQK, bSM, bPT, bPV, bTS, bTR, bEV = (
            aD.n, aM.n, aE1.n, aQK.n, aSM.n, aPT.n, aPV.n, aTS.n, aTR.n, aEV.n)

        for gi, hg in enumerate(range(4)):
            for kjt in range(ST):
                it = gi * ST + kjt
                C = -1024 * qh + 128 * kjt

                @blk.gpsimd
                def _(g):
                    if hg == 0 and kjt == 0:
                        qdone.wait(g)
                        vdone.wait(g)
                    aE1.at(g, bE1 + it * 4)
                    g.iota(Dab, pattern=[[-1, 1024]], base=C, channel_multiplier=1,
                           allow_small_or_imprecise_dtypes=True)
                    aIO.sig(g.drain())

                @blk.scalar
                def _(sc):
                    aIO.wait(sc)
                    sc.activation(Dab, Dab, AF.Abs)
                    aD.sig(sc.drain())

                for hj in range(4):
                    h = hg * 4 + hj
                    i = it * 4 + hj
                    par = i % 2
                    ft, row = h // 2, 64 * (h % 2)

                    @blk.scalar
                    def _(sc):
                        aD.at(sc, bD + it + 1)
                        aSM.at(sc, bSM + i)
                        sc.activation(M, Dab, AF.Exp, scale=self.lsc[:, 2 * h:2 * h + 1],
                                      bias=self.lsc[:, 32 + 2 * h:33 + 2 * h])
                        sc.activation(Mt, Dab, AF.Exp, scale=self.lsc[:, 2 * h + 1:2 * h + 2],
                                      bias=self.lsc[:, 33 + 2 * h:34 + 2 * h])
                        aE1.sig(sc.drain())

                    @blk.vector
                    def _(v):
                        aE1.at(v, bE1 + i + 1)
                        v.tensor_tensor(M, M, Mt, ALU.add)
                        aM.sig(v.drain())

                    @blk.tensor
                    def _(t):
                        if gi == 0 and kjt == 0 and hj == 0:
                            qdone.wait(t)
                            vdone.wait(t)
                            self.ch("kPE").wait(t)
                        aSM.at(t, bSM + i)
                        lhs = self.A48[row:row + 64, ft * 2048 + kjt * 128: ft * 2048 + kjt * 128 + 128]
                        rhs = self.A48[row:row + 64, 16384 + ft * 1024: 16384 + ft * 1024 + 1024]
                        t.matmul(self.P_SC[:, 0:512], lhs, rhs[:, 0:512], start=True, stop=True)
                        t.matmul(self.P_SC[:, 512:1024], lhs, rhs[:, 512:1024], start=True, stop=True)
                        aQK.sig(t.drain())

                    @blk.vector
                    def _(v):
                        aQK.at(v, bQK + i + 1)
                        aM.at(v, bM + i + 1)
                        aPT.at(v, bPT + i)
                        v.scalar_tensor_tensor(SM, self.P_SC[:, 0:1024], 240.0, M, ALU.min, ALU.mult)
                        aSM.sig(v.drain())

                    @blk.scalar
                    def _(sc):
                        aSM.at(sc, bSM + i + 1)
                        aPV.at(sc, bPV + i - 1)
                        sc.activation(PT(par), SM, AF.Exp)
                        aPT.sig(sc.drain())

                    if self.upto == "h14" and qh == 0 and hg == 3 and kjt == 0 and hj == 2:
                        st_ = self.ch("store")

                        @blk.scalar
                        def _(sc):
                            aPT.at(sc, bPT + i + 1)
                            sc.activation(self.scf(5120, 5184), self.lsc[:, :], AF.Identity)
                            self.ch("lscd").sig(sc.drain())

                        @blk.sync
                        def _(s):
                            aPT.at(s, bPT + i + 1)
                            self.ch("lscd").wait(s)
                            st_.sig(s.dma_start(self.outT[0, 512:640, 0:64], self.scf(5120, 5184)), 16)
                            st_.sig(s.dma_start(self.outT[0, 640:768, 0:64], self.scf(5184, 5248)), 16)
                            st_.sig(s.dma_start(self.outT[0, 0:128, 0:1024], M), 16)
                            st_.sig(s.dma_start(self.outT[0, 128:256, 0:1024], SM), 16)
                            st_.sig(s.dma_start(self.outT[0, 256:384, 0:1024], Dab), 16)
                            st_.sig(s.dma_start(self.outT[0, 384:512, 0:512], PT(par).bitcast(F32)), 16)
                            st_.wait(s)
                        return

                    @blk.tensor
                    def _(t):
                        aPT.at(t, bPT + i + 1)
                        if kjt == 0 and gi > 0:
                            aTS.at(t, bTS + gi * 4)
                        for qt in range(8):
                            idx = hj * 8 + qt
                            i_ = t.matmul(self.out_slot(idx), PT(par)[:, qt * 128:qt * 128 + 128],
                                          self.R34[:, kjt * 1040 + 65 * h: kjt * 1040 + 65 * h + 65],
                                          start=(kjt == 0 and idx % 7 == 0), stop=(kjt == ST - 1))
                        aPV.sig(t.drain())

            for hj in range(4):
                h = hg * 4 + hj

                @blk.vector
                def _(v):
                    aPV.at(v, bPV + (gi + 1) * 4 * ST)
                    aTR.at(v, bTR + gi * 4 + hj)
                    for qt in range(8):
                        sl_ = self.out_slot(hj * 8 + qt)
                        v.reciprocal(self.rnrm[:, qt:qt + 1], sl_[:, 64:65])
                        v.drain()
                        v.tensor_scalar(self.anat[:, qt * 64:qt * 64 + 64], sl_[:, 0:64],
                                        self.rnrm[:, qt:qt + 1], None, ALU.mult)
                    aTS.sig(v.drain())

                @blk.tensor
                def _(t):
                    aTS.at(t, bTS + gi * 4 + hj + 1)
                    aEV.at(t, bEV + gi * 4 + hj)
                    for qt in range(8):
                        i_ = t.transpose(self.P_TR[0:64, qt * 128:qt * 128 + 128],
                                         self.anat[:, qt * 64:qt * 64 + 64], self.idn[:])
                    aTR.sig(t.drain())

                @blk.scalar
                def _(sc):
                    aTR.at(sc, bTR + gi * 4 + hj + 1)
                    row = 64 * (h % 2)
                    dst = self.XB[row:row + 64, (h // 2) * 2048 + qh * 1024: (h // 2) * 2048 + qh * 1024 + 1024]
                    sc.activation(dst, self.P_TR[0:64, 0:1024], AF.Identity)
                    aEV.sig(sc.drain())


    def dump_bf16(self, sq, src_fn, wait_list, width=S):
        blk = self.blk
        wait_list = [w if isinstance(w, Chan) else self.ch(w) for w in wait_list]
        st = self.ch("store")

        @blk.gpsimd
        def _(g):
            for c in wait_list:
                c.wait(g)
            for et in range(ET):
                st.sig(g.dma_start(self.outT[sq, et * 128:(et + 1) * 128, 0:width], src_fn(et)), 16)
            st.wait(g)


    def layer_attn(self, sq, li):
        blk = self.blk
        t = "Ac"
        cW = self.ch(t + "W")
        cA = self.ch(t + "a")
        cV = self.ch(t + "v")
        cT = self.ch(t + "t")
        prev = self.ch("Fcz")

        @blk.gpsimd
        def _(g):
            prev.wait(g)
            cV.wait(g)
            cA.wait(g)
            for et in range(ET):
                cW.sig(g.dma_start(self.A48[:, et * 3072:(et + 1) * 3072],
                                   self.Wqkv_d[li, et * 128:(et + 1) * 128, :]), 16)

        @blk.sync
        def _(s):
            prev.wait(s)
            cV.wait(s)
            cA.wait(s)
            for j, nm in enumerate(["tcq", "tsq", "tck", "tsk"]):
                cW.sig(s.dma_start(self.scf(j * 1024, j * 1024 + 1024).rearrange("p (st d) -> p st d", st=ST),
                                   self.Td[nm].rearrange("(st p) d -> p st d", p=128)), 16)
            cW.sig(s.dma_start(self.scf(4096, 4160), self.qs_d[li, :, :]), 16)
            cW.sig(s.dma_start(self.scf(4160, 4224), self.ks_d[li, :, :]), 16)
            cW.sig(s.dma_start(self.scf(4224, 4288), self.qsr_d[li, :, :]), 16)
            cW.sig(s.dma_start(self.scf(4288, 4352), self.ksr_d[li, :, :]), 16)
            cW.sig(s.dma_start(self.lb[:, 0:8], self.lng_d[li, :, :]), 16)
            cW.sig(s.dma_start(self.lb[:, 8:16], self.lnb_d[li, :, :]), 16)
            cW.sig(s.dma_start(self.lb[:, 16:24], self.b2_d[li, :, :]), 16)
            cW.sig(s.dma_start(self.lb[:, 24:46], self.b1a_d[li, :, :]), 16)
            cW.sig(s.dma_start(self.lb[:, 46:68], self.b1g_d[li, :, :]), 16)

        @blk.vector
        def _(v):
            cW.wait(v)
            ct3 = self.ctab[:, :].rearrange("p (st tb d) -> p st tb d", st=ST, tb=4, d=D)
            for tb, so in [(0, 4096), (1, 4224), (2, 4160), (3, 4288)]:
                stg = self.scf(tb * 1024, tb * 1024 + 1024).rearrange("p (st d) -> p st d", st=ST)
                v.tensor_tensor(ct3[:, :, tb, :], stg,
                                self.scf(so, so + 64).unsqueeze(1).broadcast_to([128, ST, D]), ALU.mult)
            va = self.R34[:, 6144:6144 + 4 * 1040].rearrange("p (j h c) -> p j h c", j=4, h=H, c=65)
            v.memset(va[:, :, :, 64:65], 1.0)
            cV.sig(v.drain())

        if self.upto == "l1rg":
            self._ila_st(sq, li, 0)
            st_ = self.ch("store")

            @blk.gpsimd
            def _(g):
                cA.wait(g)
                cV.wait(g)
                cT.wait(g)
                st_.sig(g.dma_start(self.outT[sq, 0:128, 0:1024], self.R34[:, 0:1024]), 16)
                st_.sig(g.dma_start(self.outT[sq, 128:256, 0:1024], self.R34[:, 4096:5120]), 16)
                st_.sig(g.dma_start(self.outT[sq, 256:384, 0:1040], self.R34[:, 6144:7184]), 16)
                st_.sig(g.dma_start(self.outT[sq, 384:512, 0:1024], self.SCb[:, 8192:9216]), 16)
                st_.sig(g.dma_start(self.outT[sq, 512:640, 0:1024], self.R34[:, 10304:11328]), 16)
                st_.wait(g)
            return

        for st in range(ST):
            self._ila_st(sq, li, st)

        if self.upto == "l1ao":
            st_ = self.ch("store")

            @blk.gpsimd
            def _(g):
                cA.wait(g)
                cV.wait(g)
                cT.wait(g)
                for et in range(ET):
                    st_.sig(g.dma_start(self.outT[sq, et * 128:(et + 1) * 128, :], self.xbt(et)), 16)
                st_.wait(g)
            return

        @blk.gpsimd
        def _(g):
            cV.wait(g)
            cT.wait(g)
            cW.sig(self.wdma_w(g, self.Wof_d[li], 0, 0, 512), 16)
            cW.sig(self.wdma_w(g, self.Wof_d[li], 512, 4096, 512), 16)

        cOM, cOE = self.ch(t + "om"), self.ch(t + "oe")
        bOM, bOE, bW2 = cOM.n, cOE.n, cW.n

        @blk.tensor
        def _(t_):
            t_.wait_ge(cW.h, bW2)
            cA.wait(t_)
            for ft in range(ET):
                cOE.at(t_, bOE + ft)
                for et in range(ET):
                    wch, col = (0, ft * 128) if ft < 4 else (1, ft * 128 - 512)
                    lhs = self.SCb[:, wch * 4096 + et * 512 + col: wch * 4096 + et * 512 + col + 128]
                    t_.matmul(self.P_SC[:, 0:512], lhs, self.xbt(et, slice(0, 512)),
                              start=(et == 0), stop=(et == ET - 1))
                    t_.matmul(self.P_SC[:, 512:1024], lhs, self.xbt(et, slice(512, 1024)),
                              start=(et == 0), stop=(et == ET - 1))
                    t_.matmul(self.P_OUT[:, 0:512], lhs, self.xbt(et, slice(1024, 1536)),
                              start=(et == 0), stop=(et == ET - 1))
                    i_ = t_.matmul(self.P_OUT[:, 512:1024], lhs, self.xbt(et, slice(1536, 2048)),
                                   start=(et == 0), stop=(et == ET - 1))
                cOM.sig(t_.drain())

        @blk.vector
        def _(v):
            for ft in range(ET):
                cOM.at(v, bOM + ft + 1)
                v.tensor_tensor(self.zt(ft, slice(0, 1024)), self.zt(ft, slice(0, 1024)),
                                self.P_SC[:, 0:1024], ALU.add)
                v.tensor_tensor(self.zt(ft, slice(1024, 2048)), self.zt(ft, slice(1024, 2048)),
                                self.P_OUT[:, 0:1024], ALU.add)
                cOE.sig(v.drain())

    def wdma_w(self, g, dram2d, c0, dstoff_b, width):
        dst = self.SCb[:, dstoff_b:dstoff_b + ET * width].rearrange("p (a f) -> p a f", a=ET)
        src = dram2d[:, c0:c0 + width].rearrange("(a p) f -> p a f", p=128)
        return g.dma_start(dst, src)

    def _ila_st(self, sq, li, st):
        blk = self.blk
        t = "Ac"
        cW, cA, cV, cT = self.ch(t + "W"), self.ch(t + "a"), self.ch(t + "v"), self.ch(t + "t")
        t1, t2, t3 = self.scf(0, 1024), self.scf(1024, 2048), self.scf(2048, 3072)
        sqb = self.scb(6144, 7168)
        zs = self.R34[:, 10304 + (st % 2) * 1024: 10304 + (st % 2) * 1024 + 1024]

        if st == 0:
            @blk.vector
            def _(v):
                cW.wait(v)
                for et in range(4, ET):
                    v.memset(self.R34[:, 10304 + et * 128: 10304 + et * 128 + 1], 0.0)
                cV.sig(v.drain())

        @blk.scalar
        def _(sc):
            cV.wait(sc)
            cT.wait(sc)
            for et in range(ET):
                dst = zs[:, et * 128:(et + 1) * 128]
                if et < 4:
                    sc.activation(dst, self.zt(et, slice(st * 128, st * 128 + 128)), AF.Identity)
                elif st == 0:
                    sc.activation(dst[:, 1:128], self.z[:, et * S: et * S + 127], AF.Identity)
                else:
                    sc.activation(dst, self.z[:, et * S + st * 128 - 1: et * S + st * 128 + 127], AF.Identity)
            cA.sig(sc.drain())

        @blk.tensor
        def _(t_):
            cA.wait(t_)
            cV.wait(t_)
            for et in range(ET):
                lhs = zs[:, et * 128:(et + 1) * 128]
                for part, ps in [(0, self.P_SC[:, 0:1024]), (1, self.P_OUT[:, 0:1024]), (2, self.P_OUT[:, 1024:2048])]:
                    base = et * 3072 + part * 1024
                    t_.matmul(ps[:, 0:512], lhs, self.A48[:, base:base + 512],
                              start=(et == 0), stop=(et == ET - 1))
                    i_ = t_.matmul(ps[:, 512:1024], lhs, self.A48[:, base + 512:base + 1024],
                                   start=(et == 0), stop=(et == ET - 1))
            cT.sig(t_.drain())

        for ti, (ps, soff, ctb) in enumerate([(self.P_SC[:, 0:1024], 4096, 0), (self.P_OUT[:, 0:1024], 4160, 2)]):
            rot_dst = self.scb(7168 + 0, 7168 + 1024) if False else self.scb(8192 + ti * 1024, 8192 + (ti + 1) * 1024)

            @blk.scalar
            def _(sc):
                cT.wait(sc)
                cV.wait(sc)
                sc.activation(sqb, ps, AF.Square)
                cA.sig(sc.drain())

            @blk.vector
            def _(v):
                cA.wait(v)
                v.tensor_reduce(self.rnrm[:, 32:48], sqb.rearrange("p (h d) -> p h d", h=H),
                                mybir.AxisListType.X, ALU.add)
                cV.sig(v.drain())

            @blk.scalar
            def _(sc):
                cV.wait(sc)
                sc.activation(self.rnrm[:, 32:48], self.rnrm[:, 32:48], AF.Sqrt)
                cA.sig(sc.drain())

            @blk.vector
            def _(v):
                cA.wait(v)
                v.tensor_scalar(self.rnrm[:, 32:48], self.rnrm[:, 32:48], 1e-12, None, ALU.max)
                v.drain()
                v.reciprocal(self.rnrm[:, 32:48], self.rnrm[:, 32:48])
                v.drain()
                c1 = self.ctab[:, st * 256 + ctb * 64: st * 256 + ctb * 64 + 64]
                c2 = self.ctab[:, st * 256 + (ctb + 1) * 64: st * 256 + (ctb + 1) * 64 + 64]
                ps3 = ps.rearrange("p (h d) -> p h d", h=H)
                v.tensor_tensor(t1.rearrange("p (h d) -> p h d", h=H), ps3,
                                c1.unsqueeze(1).broadcast_to([128, H, D]), ALU.mult)
                v.tensor_tensor(t2.rearrange("p (h d) -> p h d", h=H)[:, :, 0:32],
                                ps.rearrange("p (h d) -> p h d", h=H)[:, :, 32:64],
                                c2.unsqueeze(1).broadcast_to([128, H, D])[:, :, 0:32], ALU.mult)
                v.tensor_tensor(t2.rearrange("p (h d) -> p h d", h=H)[:, :, 32:64],
                                ps.rearrange("p (h d) -> p h d", h=H)[:, :, 0:32],
                                c2.unsqueeze(1).broadcast_to([128, H, D])[:, :, 32:64], ALU.mult)
                v.drain()
                v.tensor_tensor(t3, t1, t2, ALU.add)
                v.drain()
                for h in range(H):
                    v.tensor_scalar(rot_dst[:, h * 64:(h + 1) * 64], t3[:, h * 64:(h + 1) * 64],
                                    self.rnrm[:, 32 + h:33 + h], None, ALU.mult)
                cV.sig(v.drain())

        @blk.scalar
        def _(sc):
            cT.wait(sc)
            vslot = self.R34[:, 6144 + (st % 4) * 1040: 6144 + (st % 4) * 1040 + 1040]
            dst = vslot.rearrange("p (h c) -> p h c", h=H, c=65)[:, :, 0:64]
            sc.activation(dst, self.P_OUT[:, 1024:2048].rearrange("p (h c) -> p h c", h=H, c=64), AF.Identity)
            cA.sig(sc.drain())

        for ti, slot_ap in enumerate([self.R34[:, 4096:5120],
                                      self.R34[:, (st % 4) * 1024: (st % 4) * 1024 + 1024]]):
            @blk.tensor
            def _(t_):
                cV.wait(t_)
                cA.wait(t_)
                src = self.scb(8192 + ti * 1024, 8192 + (ti + 1) * 1024)
                for et in range(ET):
                    i_ = t_.transpose(self.P_TR[:, et * 128:(et + 1) * 128],
                                      src[:, et * 128:(et + 1) * 128], self.idn[:])
                cT.sig(t_.drain())

            @blk.scalar
            def _(sc):
                cT.wait(sc)
                sc.activation(slot_ap, self.P_TR[:, 0:1024], AF.Identity)
                cA.sig(sc.drain())

        self._ila_attn(sq, st)

    def _ila_attn(self, sq, qt):
        blk = self.blk
        t = "Ac"
        cA, cV, cT = self.ch(t + "a"), self.ch(t + "v"), self.ch(t + "t")
        kjs = [k for k in (qt - 2, qt - 1, qt) if k >= 0]
        r0 = 3 - len(kjs)
        PTl = lambda par: self.scb(7168 + par * 512, 7168 + par * 512 + 384)
        for h in range(H):
            ft, row = h // 2, 64 * (h % 2)
            sc_ps = self.P_OUT[:, 2048:2432]

            @blk.tensor
            def _(t_):
                cA.wait(t_)
                for j, kj in enumerate(kjs):
                    r = r0 + j
                    kslot = self.R34[:, (kj % 4) * 1024: (kj % 4) * 1024 + 1024]
                    lhs = kslot[row:row + 64, ft * 128: ft * 128 + 128]
                    rhs = self.R34[row:row + 64, 4096 + ft * 128: 4096 + ft * 128 + 128]
                    i_ = t_.matmul(sc_ps[:, r * 128:(r + 1) * 128], lhs, rhs, start=True, stop=True)
                cT.sig(t_.drain())

            @blk.scalar
            def _(sc):
                cT.wait(sc)
                sc.activation(PTl(h % 2)[:, r0 * 128:384], sc_ps[:, r0 * 128:384], AF.Exp, scale=8.0)
                cA.sig(sc.drain())

            @blk.vector
            def _(v):
                cA.wait(v)
                if r0 <= 0:
                    v.tensor_tensor(PTl(h % 2)[:, 0:128], PTl(h % 2)[:, 0:128], self.tri[:, 0:128], ALU.mult)
                v.tensor_tensor(PTl(h % 2)[:, 256:384], PTl(h % 2)[:, 256:384], self.tri[:, 128:256], ALU.mult)
                cV.sig(v.drain())

            @blk.tensor
            def _(t_):
                cV.wait(t_)
                slot = self.out_slot_local(h)
                for j, kj in enumerate(kjs):
                    r = r0 + j
                    vslot = self.R34[:, 6144 + (kj % 4) * 1040: 6144 + (kj % 4) * 1040 + 1040]
                    i_ = t_.matmul(slot, PTl(h % 2)[:, r * 128:(r + 1) * 128],
                                   vslot[:, 65 * h: 65 * h + 65],
                                   start=(j == 0 and h % 7 == 0), stop=(j == len(kjs) - 1))
                cT.sig(t_.drain())

            @blk.vector
            def _(v):
                cT.wait(v)
                slot = self.out_slot_local(h)
                v.reciprocal(self.rnrm[:, 63:64], slot[:, 64:65])
                v.drain()
                v.tensor_scalar(self.anat[:, (h % 8) * 64: (h % 8) * 64 + 64], slot[:, 0:64],
                                self.rnrm[:, 63:64], None, ALU.mult)
                cV.sig(v.drain())

            if h % 8 == 7:
                @blk.tensor
                def _(t_):
                    cV.wait(t_)
                    cA.wait(t_)
                    for j in range(8):
                        i_ = t_.transpose(self.P_TR[0:64, j * 128:(j + 1) * 128],
                                          self.anat[:, j * 64:(j + 1) * 64], self.idn[:])
                    cT.sig(t_.drain())

                @blk.scalar
                def _(sc):
                    cT.wait(sc)
                    hb = h - 7
                    for par in range(2):
                        for k in range(4):
                            hh = hb + 2 * k + par
                            d1 = self.XB[64 * par:64 * par + 64,
                                         (hh // 2) * 2048 + qt * 128: (hh // 2) * 2048 + qt * 128 + 128]
                            sc.activation(d1, self.P_TR[0:64, (2 * k + par) * 128: (2 * k + par) * 128 + 128],
                                          AF.Identity)
                    cA.sig(sc.drain())

    def out_slot_local(self, h):
        bank, off = divmod(h, 7)
        regions = [self.P_SC[:, 0:512], self.P_SC[:, 512:1024], self.P_OUT[:, 0:512]]
        return regions[bank][:, off * 65: off * 65 + 65]


    def spill_ln3(self, sq):
        blk = self.blk
        t = "Fc"
        cSP, cA, cV, cT, cZ = (self.ch(t + "sp"), self.ch(t + "a"), self.ch(t + "v"),
                               self.ch(t + "t"), self.ch(t + "z"))
        wo_done = self.ch("oGZ")

        @blk.sync
        def _(s):
            wo_done.wait(s)
            for et in range(ET):
                cSP.sig(s.dma_start(self.x1sp[sq, et * 128:(et + 1) * 128, :], self.zt(et)), 16)

        zcb, sqb = self.scb(8192, 9216), self.scb(9216, 10240)
        nm_ap, rs_ap, tmp_ap = self.scf(2048, 3072), self.scf(3072, 4096), self.scf(0, 1024)
        for qh in range(QH):
            sl = slice(qh * SH, (qh + 1) * SH)
            b_a, b_t, b_v = cA.n, cT.n, cV.n

            @blk.scalar
            def _(sc):
                wo_done.wait(sc)
                cV.wait(sc)
                for et in range(ET):
                    cT.at(sc, b_t + et)
                    sc.activation(zcb, self.zt(et, sl), AF.Identity)
                    sc.activation(sqb, self.zt(et, sl), AF.Square)
                    cA.sig(sc.drain(), 2)

            @blk.tensor
            def _(t_):
                for et in range(ET):
                    t_.wait_ge(cA.h, b_a + 2 * et + 2)
                    t_.matmul(self.P_SC[:, 0:512], self.ones_bf[:], zcb[:, 0:512], start=(et == 0), stop=(et == ET - 1))
                    t_.matmul(self.P_SC[:, 512:1024], self.ones_bf[:], zcb[:, 512:1024], start=(et == 0), stop=(et == ET - 1))
                    t_.matmul(self.P_OUT[:, 0:512], self.ones_bf[:], sqb[:, 0:512], start=(et == 0), stop=(et == ET - 1))
                    t_.matmul(self.P_OUT[:, 512:1024], self.ones_bf[:], sqb[:, 512:1024], start=(et == 0), stop=(et == ET - 1))
                    cT.sig(t_.drain())

            @blk.vector
            def _(v):
                cT.at(v, b_t + ET)
                v.tensor_scalar(nm_ap, self.P_SC[:, 0:1024], -1.0 / E, None, ALU.mult)
                v.drain()
                v.tensor_tensor(rs_ap, nm_ap, nm_ap, ALU.mult)
                v.drain()
                v.scalar_tensor_tensor(rs_ap, self.P_OUT[:, 0:1024], 1.0 / E, rs_ap, ALU.mult, ALU.subtract)
                cV.sig(v.drain())

            @blk.scalar
            def _(sc):
                cV.at(sc, b_v + 1)
                sc.activation(rs_ap, rs_ap, AF.Sqrt, bias=self.epsc[:, 0:1])
                cA.sig(sc.drain())

            @blk.vector
            def _(v):
                cA.wait(v)
                cSP.wait(v)
                v.reciprocal(rs_ap, rs_ap)
                v.drain()
                for et in range(ET):
                    v.tensor_tensor(tmp_ap, self.zt(et, sl), nm_ap, ALU.add)
                    v.drain()
                    v.tensor_tensor(self.zt(et, sl), tmp_ap, rs_ap, ALU.mult)
                cV.sig(v.drain())

        @blk.vector
        def _(v):
            cZ.sig(v.drain())

    def layer_ffn(self, sq, li):
        blk = self.blk
        t = "Fc"
        cA, cV, cT, cZ = self.ch(t + "a"), self.ch(t + "v"), self.ch(t + "t"), self.ch(t + "z")
        cWF = self.ch(t + "wf")
        aA, aV, aT = self.ch("Aca"), self.ch("Acv"), self.ch("Act")
        aOE = self.ch("Acoe")
        zcb, sqb = self.scb(8192, 9216), self.scb(9216, 10240)
        nm_ap, rs_ap = self.R34f[:, 0:1024], self.R34f[:, 1024:2048]
        tmp_ap = self.R34f[:, 2048:3072]
        gt_ap = self.scb(8192, 9216)

        for qh in range(QH):
            sl = slice(qh * SH, (qh + 1) * SH)
            b_a, b_t, b_v = cA.n, cT.n, cV.n

            if qh == 0:
                @blk.vector
                def _(v):
                    aV.wait(v)
                    aT.wait(v)
                    aA.wait(v)
                    aOE.wait(v)
                    for et in range(4, ET):
                        v.memset(self.xbt(et, slice(0, 1)), 0.0)
                    cV.sig(v.drain())
                b_v = cV.n

            @blk.scalar
            def _(sc):
                aV.wait(sc)
                aT.wait(sc)
                aOE.wait(sc)
                cV.wait(sc)
                for et in range(ET):
                    cT.at(sc, b_t + et)
                    dst = self.xbt(et, sl)
                    if et < 4:
                        sc.activation(dst, self.zt(et, sl), AF.Identity)
                    elif qh == 0:
                        sc.activation(dst[:, 1:SH], self.z[:, et * S: et * S + SH - 1], AF.Identity)
                    else:
                        sc.activation(dst, self.z[:, et * S + SH - 1: et * S + 2 * SH - 1], AF.Identity)
                    sc.activation(sqb, dst, AF.Square)
                    cA.sig(sc.drain(), 2)

            @blk.tensor
            def _(t_):
                for et in range(ET):
                    t_.wait_ge(cA.h, b_a + 2 * et + 2)
                    zc2 = self.xbt(et, sl)
                    t_.matmul(self.P_SC[:, 0:512], self.ones_bf[:], zc2[:, 0:512], start=(et == 0), stop=(et == ET - 1))
                    t_.matmul(self.P_SC[:, 512:1024], self.ones_bf[:], zc2[:, 512:1024], start=(et == 0), stop=(et == ET - 1))
                    t_.matmul(self.P_OUT[:, 0:512], self.ones_bf[:], sqb[:, 0:512], start=(et == 0), stop=(et == ET - 1))
                    t_.matmul(self.P_OUT[:, 512:1024], self.ones_bf[:], sqb[:, 512:1024], start=(et == 0), stop=(et == ET - 1))
                    cT.sig(t_.drain())

            @blk.vector
            def _(v):
                cT.at(v, b_t + ET)
                v.tensor_scalar(nm_ap, self.P_SC[:, 0:1024], -1.0 / E, None, ALU.mult)
                v.drain()
                v.tensor_tensor(rs_ap, nm_ap, nm_ap, ALU.mult)
                v.drain()
                v.scalar_tensor_tensor(rs_ap, self.P_OUT[:, 0:1024], 1.0 / E, rs_ap, ALU.mult, ALU.subtract)
                cV.sig(v.drain())

            @blk.scalar
            def _(sc):
                cV.at(sc, b_v + 1)
                sc.activation(rs_ap, rs_ap, AF.Sqrt, bias=self.epsc[:, 0:1])
                cA.sig(sc.drain())

            @blk.vector
            def _(v):
                cA.wait(v)
                v.reciprocal(rs_ap, rs_ap)
                cV.sig(v.drain())

            for et in range(ET):
                @blk.vector
                def _(v):
                    cV.wait(v)
                    cA.wait(v)
                    v.tensor_tensor(tmp_ap, self.xbt(et, sl), nm_ap, ALU.add)
                    v.drain()
                    v.tensor_tensor(tmp_ap, tmp_ap, rs_ap, ALU.mult)
                    cV.sig(v.drain())

                @blk.scalar
                def _(sc):
                    cV.wait(sc)
                    sc.activation(self.xbt(et, sl), tmp_ap, AF.Identity,
                                  scale=self.lb[:, et:et + 1], bias=self.lb[:, 8 + et:9 + et])
                    cA.sig(sc.drain())

        if self.upto == "l1zn":
            st_ = self.ch("store")

            @blk.gpsimd
            def _(g):
                cA.wait(g)
                cV.wait(g)
                for et in range(ET):
                    st_.sig(g.dma_start(self.outT[sq, et * 128:(et + 1) * 128, :], self.xbt(et)), 16)
                st_.wait(g)
            return

        for sc2 in range(2):
            ssl = slice(sc2 * 1024, sc2 * 1024 + 1024)
            b_wf, b_t, b_v, b_a = cWF.n, cT.n, cV.n, cA.n
            wu = self.ch(t + "wu")
            b_wu = wu.n

            @blk.gpsimd
            def _(g):
                cA.wait(g)
                cV.wait(g)
                cT.wait(g)
                for ct in range(CT):
                    w = min(128, CI - ct * 128)
                    if ct >= 2:
                        wu.at(g, b_wu + ct - 1)
                    cWF.sig(self.wdma_w(g, self.W1_d[li], ct * 128, (2 * ct % 4) * 2048, w), 16)
                    cWF.sig(self.wdma_w(g, self.W1_d[li], CI + ct * 128, ((2 * ct + 1) % 4) * 2048, w), 16)

            for ct in range(CT):
                w = min(128, CI - ct * 128)

                @blk.tensor
                def _(t_):
                    t_.wait_ge(cWF.h, b_wf + 32 * (ct + 1))
                    cV.at(t_, b_v + ct)
                    cA.at(t_, b_a + ct)
                    for part, ps in [(0, self.P_SC[:, 0:1024]), (1, self.P_OUT[:, 0:1024])]:
                        so = ((2 * ct + part) % 4) * 2048
                        for et in range(ET):
                            lhs = self.SCb[:, so + et * w: so + et * w + w]
                            t_.matmul(ps[0:w, 0:512], lhs, self.xbt(et, slice(ssl.start, ssl.start + 512)),
                                      start=(et == 0), stop=(et == ET - 1))
                            i_ = t_.matmul(ps[0:w, 512:1024], lhs, self.xbt(et, slice(ssl.start + 512, ssl.start + 1024)),
                                           start=(et == 0), stop=(et == ET - 1))
                    cT.sig(t_.drain())
                    wu.sig(t_.drain())

                @blk.scalar
                def _(sc):
                    cT.at(sc, b_t + ct + 1)
                    sc.activation(gt_ap[0:w, :], self.P_OUT[0:w, 0:1024], AF.Gelu,
                                  bias=self.lb[0:w, 46 + ct:47 + ct])
                    cA.sig(sc.drain())

                @blk.vector
                def _(v):
                    cA.at(v, b_a + ct + 1)
                    v.scalar_tensor_tensor(self.A48[0:w, ct * 1024:(ct + 1) * 1024], self.P_SC[0:w, 0:1024],
                                           self.lb[0:w, 24 + ct:25 + ct], gt_ap[0:w, :], ALU.add, ALU.mult)
                    cV.sig(v.drain())

            if self.upto == "l1h2" and sc2 == 0:
                st_ = self.ch("store")

                @blk.gpsimd
                def _(g):
                    cV.wait(g)
                    cA.wait(g)
                    for r in range(11):
                        dsti = self.outT[0, r * 128:(r + 1) * 128, 0:2048] if r < 8 else \
                               self.outT[1, (r - 8) * 128:(r - 7) * 128, 0:2048]
                        st_.sig(g.dma_start(dsti, self.A48[:, r * 2048:(r + 1) * 2048]), 16)
                    st_.wait(g)
                return

            NFG, NK4 = 2, 4
            for fg in range(NFG):
                b_wf2, b_t2, b_v2 = cWF.n, cT.n, cV.n
                w2u = self.ch(t + "w2u")
                b_w2u = w2u.n

                @blk.gpsimd
                def _(g):
                    cV.at(g, b_v2)
                    cT.wait(g)
                    for ct in range(CT):
                        w = min(128, CI - ct * 128)
                        if ct >= 4:
                            w2u.at(g, b_w2u + ct - 3)
                        dst = self.SCb[:, (ct % 4) * 1024: (ct % 4) * 1024 + 1024]
                        cWF.sig(g.dma_start(dst[0:w, :], self.W2_d[li, ct * 128: ct * 128 + w, :]), 16)

                ps4 = [self.P_SC[:, 0:1024], self.P_OUT[:, 0:1024], self.P_OUT[:, 1024:2048], None]

                @blk.tensor
                def _(t_):
                    for ct in range(CT):
                        w = min(128, CI - ct * 128)
                        t_.wait_ge(cWF.h, b_wf2 + 16 * (ct + 1))
                        if ct == 0:
                            cV.at(t_, b_v2)
                        slot = self.SCb[:, (ct % 4) * 1024: (ct % 4) * 1024 + 1024]
                        for k4 in range(NK4):
                            ft = fg * NK4 + k4
                            lhs = slot[0:w, ft * 128: ft * 128 + 128]
                            rhs0 = self.A48[0:w, ct * 1024 + 0: ct * 1024 + 512]
                            rhs1 = self.A48[0:w, ct * 1024 + 512: ct * 1024 + 1024]
                            if k4 < 3:
                                t_.matmul(ps4[k4][:, 0:512], lhs, rhs0, start=(ct == 0), stop=(ct == CT - 1))
                                i_ = t_.matmul(ps4[k4][:, 512:1024], lhs, rhs1, start=(ct == 0), stop=(ct == CT - 1))
                            else:
                                t_.matmul(self.P_OUT[:, 2048:2560], lhs, rhs0, start=(ct == 0), stop=(ct == CT - 1))
                                i_ = t_.matmul(self.P_Xf[:, 0:512], lhs, rhs1, start=(ct == 0), stop=(ct == CT - 1))
                        w2u.sig(t_.drain())
                    cT.sig(t_.drain())

                @blk.vector
                def _(v):
                    cT.at(v, b_t2 + 1)
                    for k4 in range(NK4):
                        ft = fg * NK4 + k4
                        bcol = self.lb[:, 16 + ft:17 + ft]
                        if k4 < 3:
                            v.scalar_tensor_tensor(self.zt(ft, ssl), ps4[k4][:, 0:1024], bcol,
                                                   self.zt(ft, ssl), ALU.add, ALU.add)
                        else:
                            v.scalar_tensor_tensor(self.zt(ft, slice(ssl.start, ssl.start + 512)),
                                                   self.P_OUT[:, 2048:2560], bcol,
                                                   self.zt(ft, slice(ssl.start, ssl.start + 512)), ALU.add, ALU.add)
                            v.scalar_tensor_tensor(self.zt(ft, slice(ssl.start + 512, ssl.start + 1024)),
                                                   self.P_Xf[:, 0:512], bcol,
                                                   self.zt(ft, slice(ssl.start + 512, ssl.start + 1024)), ALU.add, ALU.add)
                        v.drain()
                    cV.sig(v.drain())

        @blk.vector
        def _(v):
            cZ.sig(v.drain())

    def final_ln(self, sq):
        blk = self.blk
        t = "Fc"
        cA, cV, cT, cZ = self.ch(t + "a"), self.ch(t + "v"), self.ch(t + "t"), self.ch(t + "z")
        cX = self.ch(t + "x1")
        st_ = self.ch("store")
        zcb, sqb = self.scb(8192, 9216), self.scb(9216, 10240)
        nm_ap, rs_ap = self.scf(2048, 3072), self.scf(3072, 4096)
        x1b = lambda i: self.scf((i % 2) * 1024, (i % 2) * 1024 + 1024)
        for qh in range(QH):
            sl = slice(qh * SH, (qh + 1) * SH)
            b_a, b_t, b_v, b_x = cA.n, cT.n, cV.n, cX.n

            @blk.scalar
            def _(sc):
                cZ.wait(sc)
                cV.wait(sc)
                for et in range(ET):
                    cT.at(sc, b_t + et)
                    sc.activation(zcb, self.zt(et, sl), AF.Identity)
                    sc.activation(sqb, self.zt(et, sl), AF.Square)
                    cA.sig(sc.drain(), 2)

            @blk.tensor
            def _(t_):
                cZ.wait(t_)
                for et in range(ET):
                    t_.wait_ge(cA.h, b_a + 2 * et + 2)
                    t_.matmul(self.P_SC[:, 0:512], self.ones_bf[:], zcb[:, 0:512], start=(et == 0), stop=(et == ET - 1))
                    t_.matmul(self.P_SC[:, 512:1024], self.ones_bf[:], zcb[:, 512:1024], start=(et == 0), stop=(et == ET - 1))
                    t_.matmul(self.P_OUT[:, 0:512], self.ones_bf[:], sqb[:, 0:512], start=(et == 0), stop=(et == ET - 1))
                    t_.matmul(self.P_OUT[:, 512:1024], self.ones_bf[:], sqb[:, 512:1024], start=(et == 0), stop=(et == ET - 1))
                    cT.sig(t_.drain())

            @blk.vector
            def _(v):
                cT.at(v, b_t + ET)
                v.tensor_scalar(nm_ap, self.P_SC[:, 0:1024], -1.0 / E, None, ALU.mult)
                v.drain()
                v.tensor_tensor(rs_ap, nm_ap, nm_ap, ALU.mult)
                v.drain()
                v.scalar_tensor_tensor(rs_ap, self.P_OUT[:, 0:1024], 1.0 / E, rs_ap, ALU.mult, ALU.subtract)
                cV.sig(v.drain())

            @blk.scalar
            def _(sc):
                cV.at(sc, b_v + 1)
                sc.activation(rs_ap, rs_ap, AF.Sqrt, bias=self.epsc[:, 0:1])
                cA.sig(sc.drain())

            @blk.vector
            def _(v):
                cA.wait(v)
                v.reciprocal(rs_ap, rs_ap)
                cV.sig(v.drain())
            b_v2 = cV.n

            ox = self.ch(t + "ox")
            b_ox, b_st = ox.n, st_.n
            for et in range(ET):
                @blk.sync
                def _(s):
                    if et < 2:
                        cV.at(s, b_v2)
                    else:
                        st_.at(s, b_st + 16 * (et - 1))
                    cX.sig(s.dma_start(x1b(et), self.x1sp[sq, et * 128:(et + 1) * 128, sl]), 16)

                @blk.vector
                def _(v):
                    cX.at(v, b_x + 16 * (et + 1))
                    tmp2 = self.R34f[:, 2048:3072]
                    v.tensor_tensor(tmp2, self.zt(et, sl), nm_ap, ALU.add)
                    v.drain()
                    v.tensor_tensor(tmp2, tmp2, rs_ap, ALU.mult)
                    v.drain()
                    v.tensor_tensor(x1b(et), x1b(et), tmp2, ALU.add)
                    ox.sig(v.drain())

                @blk.sync
                def _(s):
                    ox.at(s, b_ox + et + 1)
                    st_.sig(s.dma_start(self.outT[sq, et * 128:(et + 1) * 128, sl], x1b(et)), 16)

        @blk.sync
        def _(s):
            st_.wait(s)

    def _wo_ln_res(self, sq):
        blk = self.blk
        gW, mm, mmS, ev = self.ch("oGW"), self.ch("oGM"), self.ch("oGT"), self.ch("oGE")
        stt, wZ = self.ch("oGS"), self.ch("oGZ")
        a0ev, a1ev, a1sm = self.ch("ath0ev"), self.ch("ath1ev"), self.ch("ath1sm")
        a0pv, a1pv = self.ch("ath0pv"), self.ch("ath1pv")
        zcb, sqb = self.scb(8192, 9216), self.scb(9216, 10240)
        nm_ap, rs_ap = self.R34f[:, 0:1024], self.R34f[:, 1024:2048]
        tmpf, tmpf2 = self.R34f[:, 2048:3072], self.R34f[:, 3072:4096]
        b_gW = gW.n

        @blk.gpsimd
        def _(g):
            a0ev.wait(g)
            a1ev.wait(g)
            a1sm.wait(g)
            a0pv.wait(g)
            a1pv.wait(g)
            gW.sig(self.wdma(g, self.Wd["WoT"], 0, 0), 16)
            gW.sig(self.wdma(g, self.Wd["WoT"], 512, 1), 16)

        for qh in range(QH):
            sl = slice(qh * SH, (qh + 1) * SH)
            b_mm, b_mmS, b_ev, b_stt, b_wZ = mm.n, mmS.n, ev.n, stt.n, wZ.n
            if self.upto == "wop1" and qh == 1:
                st_ = self.ch("store")

                @blk.sync
                def _(s):
                    stt.at(s, b_stt)
                    st_.sig(s.dma_start(self.outT[0, 0:128, 0:1024], nm_ap), 16)
                    st_.sig(s.dma_start(self.outT[0, 128:256, 0:1024], rs_ap), 16)
                    st_.sig(s.dma_start(self.outT[0, 256:384, 0:1024], self.SCf[:, 0:1024]), 16)
                    st_.wait(s)
                return

            @blk.tensor
            def _(t):
                t.wait_ge(gW.h, b_gW + 32)
                if qh == 1:
                    wZ.at(t, b_wZ)
                else:
                    a1ev.wait(t)
                for ft in range(ET):
                    ev.at(t, b_ev + ft)
                    for et in range(ET):
                        wch, col = (0, ft * 128) if ft < 4 else (1, ft * 128 - 512)
                        lhs = self.wtile(wch, et, col)
                        t.matmul(self.P_SC[:, 0:512], lhs, self.xbt(et, slice(sl.start, sl.start + 512)),
                                 start=(et == 0), stop=(et == ET - 1))
                        i_ = t.matmul(self.P_SC[:, 512:1024], lhs, self.xbt(et, slice(sl.start + 512, sl.start + 1024)),
                                      start=(et == 0), stop=(et == ET - 1))
                    mm.sig(t.drain())
                    ev.at(t, b_ev + ft + 1)
                    t.matmul(self.P_OUT[:, 1024:1536], self.ones_bf[:], zcb[:, 0:512], start=(ft == 0), stop=(ft == ET - 1))
                    t.matmul(self.P_OUT[:, 1536:2048], self.ones_bf[:], zcb[:, 512:1024], start=(ft == 0), stop=(ft == ET - 1))
                    t.matmul(self.P_OUT[:, 2048:2560], self.ones_bf[:], sqb[:, 0:512], start=(ft == 0), stop=(ft == ET - 1))
                    t.matmul(self.P_Xf[:, 0:512], self.ones_bf[:], sqb[:, 512:1024], start=(ft == 0), stop=(ft == ET - 1))
                    mmS.sig(t.drain())

            @blk.scalar
            def _(sc):
                for ft in range(ET):
                    mm.at(sc, b_mm + ft + 1)
                    mmS.at(sc, b_mmS + ft)
                    bcol = self.bias4[:, 2 * ET + ft:2 * ET + ft + 1]
                    sc.activation(zcb, self.P_SC[:, 0:1024], AF.Identity, bias=bcol)
                    sc.activation(sqb, self.P_SC[:, 0:1024], AF.Square, bias=bcol)
                    ev.sig(sc.drain())

            @blk.vector
            def _(v):
                mmS.at(v, b_mmS + ET)
                v.tensor_scalar(nm_ap, self.P_OUT[:, 1024:2048], -1.0 / E, None, ALU.mult)
                v.drain()
                v.tensor_tensor(rs_ap, nm_ap, nm_ap, ALU.mult)
                v.drain()
                v.scalar_tensor_tensor(rs_ap[:, 0:512], self.P_OUT[:, 2048:2560], 1.0 / E,
                                       rs_ap[:, 0:512], ALU.mult, ALU.subtract)
                v.scalar_tensor_tensor(rs_ap[:, 512:1024], self.P_Xf[:, 0:512], 1.0 / E,
                                       rs_ap[:, 512:1024], ALU.mult, ALU.subtract)
                stt.sig(v.drain())

            @blk.scalar
            def _(sc):
                stt.at(sc, b_stt + 1)
                sc.activation(rs_ap, rs_ap, AF.Sqrt, bias=self.epsc[:, 0:1])
                stt.sig(sc.drain())

            @blk.vector
            def _(v):
                stt.at(v, b_stt + 2)
                v.reciprocal(rs_ap, rs_ap)
                stt.sig(v.drain())

            b_mm2 = mm.n

            @blk.tensor
            def _(t):
                stt.at(t, b_stt + 3)
                for ft in range(ET):
                    wZ.at(t, b_wZ + ft)
                    for et in range(ET):
                        wch, col = (0, ft * 128) if ft < 4 else (1, ft * 128 - 512)
                        lhs = self.wtile(wch, et, col)
                        t.matmul(self.P_SC[:, 0:512], lhs, self.xbt(et, slice(sl.start, sl.start + 512)),
                                 start=(et == 0), stop=(et == ET - 1))
                        i_ = t.matmul(self.P_SC[:, 512:1024], lhs, self.xbt(et, slice(sl.start + 512, sl.start + 1024)),
                                      start=(et == 0), stop=(et == ET - 1))
                    mm.sig(t.drain())

            @blk.vector
            def _(v):
                for ft in range(ET):
                    mm.at(v, b_mm2 + ft + 1)
                    v.scalar_tensor_tensor(tmpf, self.P_SC[:, 0:1024], self.bias4[:, 2 * ET + ft:2 * ET + ft + 1],
                                           nm_ap, ALU.add, ALU.add)
                    v.drain()
                    v.tensor_tensor(tmpf2, tmpf, rs_ap, ALU.mult)
                    v.drain()
                    v.tensor_tensor(self.zt(ft, sl), self.zt(ft, sl), tmpf2, ALU.add)
                    wZ.sig(v.drain())


def build_program(upto=None):
    kb = KB(upto or UPTO)
    partial = (upto or UPTO) not in ("stageG", "full") and not (upto or UPTO).startswith("layer")
    partial = partial or (upto or UPTO) in ("ln3z", "l1attn", "l1ao", "l1rg", "l1z2", "l1zn", "l1h2")
    with kb.stack:
        kb.alloc()
        kb.init_consts()
        for sq in range(1 if partial else 2):
            ld = kb.load_x(sq)
            waits = [ld] if sq == 0 else [ld, "oGZ"]
            ln0 = kb.ln_z_to_xb("l0s", waits)
            kb.stage_g(sq, ln0)
            if kb.upto in ("lsc", "ln0", "kproj", "vproj", "qproj", "attn", "wop1", "h14"):
                continue
            if kb.upto == "stageG":
                kb.store_z(sq, ["oGZ"])
                continue
            kb.spill_ln3(sq)
            if kb.upto == "ln3z":
                kb.dump_z(sq, ["Fcv", "Fcz"])
                continue
            nlayers = NL if kb.upto in ("full", "l1attn", "l1ao", "l1rg", "l1z2", "l1zn", "l1h2") else int(kb.upto[5:])
            for li in range(nlayers):
                kb.layer_attn(sq, li)
                if kb.upto in ("l1ao", "l1rg"):
                    break
                if kb.upto == "l1attn":
                    kb.dump_z(sq, ["Acv", "Aca", "Act", "Acoe"])
                    break
                kb.layer_ffn(sq, li)
                if kb.upto in ("l1z2", "l1zn", "l1h2"):
                    if kb.upto == "l1z2":
                        kb.dump_z(sq, ["Fcz", "Fcv", "Fca"])
                    break
            if kb.upto in ("l1attn", "l1ao", "l1rg", "l1z2", "l1zn", "l1h2"):
                continue
            kb.final_ln(sq)
    return kb.nc


def kernel(**inputs):
    d = _prep(inputs)
    nc = build_program()
    shared = {k: v for k, v in d.items() if not k.startswith("_")}
    in_maps = [dict(shared, xT=d["_x"][c]) for c in range(8)]
    res = run_bass_kernel_spmd(nc, in_maps, core_ids=list(range(8)))
    out = np.empty((16, S, E), np.float32)
    for c in range(8):
        out[2 * c:2 * c + 2] = res.results[c]["outT"].transpose(0, 2, 1)
    return out
